# version 20
# baseline (speedup 1.0000x reference)
import contextlib
import numpy as np
import concourse.bass as bass
import concourse.mybir as mybir
from concourse.bass_utils import run_bass_kernel_spmd

F32 = mybir.dt.float32
BF16 = mybir.dt.bfloat16
I32 = mybir.dt.int32
AF = mybir.ActivationFunctionType
ALU = mybir.AluOpType
AX = mybir.AxisListType.X

H = 16
D = 2048
TC = 2048
TE = TC + 3
CH = 128
NCH = TC // CH
NJT = D // 128
J_NEUMANN = 4
QUAKE_C = 1597463007.0
DKS = float(128) ** -0.5


def _prep_inputs(inp):
    f32 = np.float32
    C = np.ascontiguousarray
    x = inp["L0_attn_x"]

    def strips(W):
        return C(W.reshape(NJT, 128, H, 128).transpose(2, 0, 1, 3))

    qcw, kcw, vcw = inp["L0_attn_qcw"], inp["L0_attn_kcw"], inp["L0_attn_vcw"]
    qcb, kcb, vcb = inp["L0_attn_qcb"], inp["L0_attn_kcb"], inp["L0_attn_vcb"]
    shared = {
        "wq": strips(inp["L0_attn_Wq"]), "wk": strips(inp["L0_attn_Wk"]),
        "wv": strips(inp["L0_attn_Wv"]), "wg": strips(inp["L0_attn_Wg"]),
        "wo": C(inp["L0_attn_Wo"].reshape(16, 128, 16, 128)
                .transpose(2, 0, 1, 3)),
        "rpack": C(np.concatenate([inp["L0_attn_Wa"], inp["L0_attn_Wb"],
                                   inp["L0_ffn_router_weight"].T], axis=1)
                   .reshape(NJT, 128, 96)),
        "rbias": C(np.concatenate([inp["L0_attn_ba"], inp["L0_attn_bb"],
                                   inp["L0_ffn_router_bias"]])
                   .reshape(96, 1).astype(f32)),
        "eb": C(inp["L0_ffn_expert_bias"].astype(f32)),
        "convw": C(np.concatenate(
            [w.reshape(H, 128, 1, 4) for w in (qcw, kcw, vcw)], axis=2)
            .transpose(1, 0, 2, 3).reshape(128, H * 12)),
        "convb": C(np.concatenate(
            [b.reshape(H, 128, 1) for b in (qcb, kcb, vcb)], axis=2)
            .transpose(1, 0, 2).reshape(128, H * 3)),
        "lncol": C(np.stack([inp["L0_attn_ln_w"].reshape(128),
                             inp["L0_attn_ln_b"].reshape(128)], axis=1)),
        "idf": np.eye(128, dtype=f32),
        "triu_i": C(np.triu(np.ones((128, 128), f32), 0)),
        "triu_s": C(np.triu(np.ones((128, 128), f32), 1)),
        "tril_s": C(np.tril(np.ones((128, 128), f32), -1)),
        "msk": C(np.concatenate(
            [-np.triu(np.ones((128, 128), f32), 1) / 128.0,
             -np.triu(np.ones((128, 128), f32), 0) * DKS], axis=1)),
        "ones1": np.ones((1, 128), f32),
        "onescol": np.ones((128, 1), f32),
    }
    rng = np.random.default_rng()
    nonce = int(rng.integers(1 << 16, 1 << 29))
    expect = (nonce + np.arange(H)).astype(np.int32).reshape(1, H)
    in_maps = []
    for b in range(4):
        for half in range(2):
            xt = np.zeros((D, TE), f32)
            if half == 0:
                xt[:, 3:] = x[b, 0:TC, :].T
            else:
                xt[:, :] = x[b, TC - 3:2 * TC, :].T
            pv = np.array([[half, half * (H * 128 * 128),
                            (1 - half) * (H * 128 * 128),
                            half * H, (1 - half) * H, 0, 0, 0]], np.int32)
            m = {"xt": C(xt), "pvec": pv, "expect": expect,
                 "oddmask": np.full((128, 1), -1 if half else 0, np.int32)}
            m.update(shared)
            in_maps.append(m)
    return in_maps


class Sem:
    def __init__(self, stack, nc, name):
        self.h = stack.enter_context(nc.semaphore(name))
        self.n = 0
        self.name = name

    def inc(self, k=1):
        self.n += k
        return self.n


class EM:
    ENG = ("tensor", "vector", "scalar", "gpsimd", "sync")

    def __init__(self):
        self.q = {e: [] for e in self.ENG}
        self.waited = {e: {} for e in self.ENG}
        self.ready = {}

    def emit(self, eng, fn):
        self.q[eng].append(fn)

    def wait(self, eng, sem, val):
        if val <= 0:
            return
        w = self.waited[eng]
        if w.get(sem.name, 0) >= val:
            return
        w[sem.name] = val
        fn = lambda e, s=sem.h, v=val: e.wait_ge(s, v)
        fn.is_wait = True
        self.q[eng].append(fn)

    def mil(self, eng, sem):
        last = self.q[eng].pop()
        assert not getattr(last, "is_wait", False), f"mil({eng}) after wait"

        def wrapped(e, last=last, s=sem.h):
            r = last(e)
            r.then_inc(s, 1)
            return r
        self.q[eng].append(wrapped)
        return sem.inc()

    def mark(self, key, sem, val):
        self.ready[key] = (sem, val)

    def need(self, eng, *keys):
        for k in keys:
            if k in self.ready:
                s, v = self.ready[k]
                self.wait(eng, s, v)


TENSORS = {}


def build_program(debug=(), nheads=H, nchunks=NCH, detect_races=False,
                  no_stage_d=False):
    nc = bass.Bass("TRN2", target_bir_lowering=False, num_devices=8,
                   detect_race_conditions=detect_races)
    ap = lambda t: t.ap()
    xt_d = ap(nc.dram_tensor("xt", [D, TE], F32, kind="ExternalInput"))
    pvec_d = ap(nc.dram_tensor("pvec", [1, 8], I32, kind="ExternalInput"))
    expect_d = ap(nc.dram_tensor("expect", [1, H], I32, kind="ExternalInput"))
    oddm_d = ap(nc.dram_tensor("oddmask", [128, 1], I32, kind="ExternalInput"))
    w_d = {k: ap(nc.dram_tensor(k, [H, NJT, 128, 128], F32,
                                kind="ExternalInput"))
           for k in ("wq", "wk", "wv", "wg")}
    wo_d = ap(nc.dram_tensor("wo", [16, 16, 128, 128], F32, kind="ExternalInput"))
    rpack_d = ap(nc.dram_tensor("rpack", [NJT, 128, 96], F32, kind="ExternalInput"))
    rbias_d = ap(nc.dram_tensor("rbias", [96, 1], F32, kind="ExternalInput"))
    eb_d = ap(nc.dram_tensor("eb", [64, D], F32, kind="ExternalInput"))
    convw_d = ap(nc.dram_tensor("convw", [128, H * 12], F32, kind="ExternalInput"))
    convb_d = ap(nc.dram_tensor("convb", [128, H * 3], F32, kind="ExternalInput"))
    lncol_d = ap(nc.dram_tensor("lncol", [128, 2], F32, kind="ExternalInput"))
    idf_d = ap(nc.dram_tensor("idf", [128, 128], F32, kind="ExternalInput"))
    triui_d = ap(nc.dram_tensor("triu_i", [128, 128], F32, kind="ExternalInput"))
    trius_d = ap(nc.dram_tensor("triu_s", [128, 128], F32, kind="ExternalInput"))
    trils_d = ap(nc.dram_tensor("tril_s", [128, 128], F32, kind="ExternalInput"))
    msk_d = ap(nc.dram_tensor("msk", [128, 256], F32, kind="ExternalInput"))
    ones1_d = ap(nc.dram_tensor("ones1", [1, 128], F32, kind="ExternalInput"))
    onescol_d = ap(nc.dram_tensor("onescol", [128, 1], F32, kind="ExternalInput"))
    out_d = ap(nc.dram_tensor("outT", [D, TC], F32, kind="ExternalOutput"))
    og_d = ap(nc.dram_tensor("og_scr", [H * 128, TC], BF16, kind="Internal"))
    xchgA = ap(nc.dram_tensor("xchgA", [H * 128, 128], F32, kind="Internal",
                              addr_space="Shared"))
    flagA = ap(nc.dram_tensor("flagA", [1, H], I32, kind="Internal",
                              addr_space="Shared"))
    dbg = {}
    for name, shape, dt in debug:
        dbg[name] = ap(nc.dram_tensor("dbg_" + name, shape, dt,
                                      kind="ExternalOutput"))

    em = EM()
    st = contextlib.ExitStack()
    with st:
        sb = lambda n, sh, dt: st.enter_context(nc.sbuf_tensor(n, sh, dt))
        psum = lambda n, sh, dt: st.enter_context(nc.psum_tensor(n, sh, dt))

        IDF = sb("IDF", [128, 128], F32)
        IDB = sb("IDB", [128, 128], BF16)
        TRIU_I = sb("TRIU_I", [128, 128], F32)
        TRIU_S = sb("TRIU_S", [128, 128], F32)
        TRIL_S = sb("TRIL_S", [128, 128], F32)
        MSK = sb("MSK", [128, 256], BF16)
        ONES1 = sb("ONES1", [1, 128], F32)
        ONESC = sb("ONESC", [128, 1], BF16)
        LNCOL = sb("LNCOL", [128, 2], F32)
        RBIAS = sb("RBIAS", [96, 1], F32)
        CONVW = sb("CONVW", [128, H * 12], F32)
        CONVBH = sb("CONVBH", [128, H * 3], F32)
        PVEC = sb("PVEC", [1, 8], I32)
        EXPECT = sb("EXPECT", [1, H], I32)
        ODDM = sb("ODDM", [128, 1], I32)
        EBS = sb("EBS", [64, D], BF16)
        RPACK = sb("RPACK", [128, NJT * 96], BF16)
        FLG = sb("FLG", [1, 4], I32)
        ZEROC = sb("ZEROC", [128, 128], F32)
        DBGB = sb("DBGB", [128, 4 * 272], F32) if debug else None
        CB = sb("CB", [128, 2], F32)

        XBF = sb("XBF", [128, NJT * TE], BF16)
        MTS = sb("MTS", [64, TC], BF16)
        SWB = sb("SWB", [128, TC], BF16)
        MUR = sb("MUR", [1, TE], F32)
        RSR = sb("RSR", [1, TE], F32)
        LGAB = sb("LGAB", [128, NCH * 32], F32)
        ECOL = sb("ECOL", [128, NCH * 96], F32)
        ST = sb("ST", [128, 128], F32)
        STB = sb("STB", [128, 128], BF16)
        RECV = sb("RECV", [128, 128], F32)
        VARB = sb("VARB", [128, 128], F32)
        EROWB = sb("EROWB", [16, 256], F32)
        EROW2 = sb("EROW2", [1, 512], F32)
        ROWS = sb("ROWS", [1, 544], F32)

        AR = sb("ARENA", [128, 46496], BF16)

        def car(off, n):
            return AR[:, off:off + n]

        def far(off, n):
            return AR[:].bitcast(F32)[:, off:off + n]


        o = 0
        def take(n):
            nonlocal o
            r = o
            o += n
            return r
        KPRE = car(take(TE + 1), TE)
        QPRE = car(take(TE + 1), TE)
        VPRE = car(take(TE + 1), TE)
        KT = car(take(TC), TC)
        QT = car(take(TC), TC)
        VT = car(take(TC), TC)
        GTH = car(take(TC), TC)
        OGS = car(take(TC), TC)
        DIAG = car(take(12 * 128), 12 * 128)
        WSB = {k: [car(take(NJT * 128), NJT * 128) for p in range(2)]
               for k in ("wq", "wk", "wv", "wg")}
        NSLOT = 3
        PBT = [car(take(128), 128) for _ in range(NSLOT)]
        U0S = [car(take(128), 128) for _ in range(NSLOT)]
        WTS = [car(take(128), 128) for _ in range(NSLOT)]
        GKS = [car(take(128), 128) for _ in range(NSLOT)]
        RS = [car(take(256), 256) for _ in range(NSLOT)]
        APK = [car(take(256), 256) for _ in range(NSLOT)]
        XS = [car(take(256), 256) for _ in range(2)]
        ET = car(take(256), 256)
        UB = car(take(128), 128)
        OLBD = car(take(256), 256)
        THB = car(take(512), 512)
        f_off = (o + 2) // 2
        oXSQB = o
        XSQB = car(take(1040), 1040)
        THB2 = car(oXSQB, 512)
        THBD = [THB, THB2]
        MT1 = car(take(64), 64)
        f_off = (o + 2) // 2
        OT = far(f_off, 256); f_off += 256
        ECL = far(f_off, 256); f_off += 256
        SCR = far(f_off, 256); f_off += 256
        SCR2 = far(f_off, 256); f_off += 256
        ZT = far(f_off, 96); f_off += 96
        OB = far(f_off, NCH * 128); f_off += NCH * 128
        c_end = 2 * f_off
        o = 0
        OGT = car(take(16 * 2048), 16 * 2048)
        WOD = [car(take(16 * 128), 16 * 128) for _ in range(2)]
        f2 = (o + 2) // 2
        XRT = [far(f2 + i * 512, 512) for i in range(2)]; f2 += 1024
        OUTS = [far(f2 + i * 512, 512) for i in range(2)]; f2 += 1024
        FFN = [far(f2 + i * 512, 512) for i in range(2)]; f2 += 1024
        THD = far(f2, 512); f2 += 512
        d_end = 2 * f2
        assert max(c_end, d_end) <= 46496, (c_end, d_end)

        PA = psum("PA", [128, 512], F32)
        PB = psum("PB", [128, 512], F32)
        PC = psum("PC", [128, 256], F32)
        PD = psum("PD", [128, 256], F32)
        PEb = psum("PEb", [128, 256], F32)
        PF = psum("PF", [128, 256], BF16)
        PF2 = PF[:, 128:256]
        PG = psum("PG", [128, 128], F32)
        PH = psum("PH", [1, 512], F32)

        sX = Sem(st, nc, "sX")
        sWC = Sem(st, nc, "sWC")
        sXL = Sem(st, nc, "sXL")
        sWa = Sem(st, nc, "sWa")
        sWb = Sem(st, nc, "sWb")
        sOG = Sem(st, nc, "sOG")
        sXR = Sem(st, nc, "sXR")
        sT = Sem(st, nc, "sT")
        sV = Sem(st, nc, "sV")
        sS = Sem(st, nc, "sS")
        sG = Sem(st, nc, "sG")
        sO = Sem(st, nc, "sO")
        sP = Sem(st, nc, "sP")
        sE = Sem(st, nc, "sE")
        sE2 = Sem(st, nc, "sE2")
        sEA = Sem(st, nc, "sEA")
        sR = Sem(st, nc, "sR")
        sPoll = st.enter_context(nc.semaphore("sPoll"))

        E = em.emit
        W = em.wait
        NEED = em.need
        MARK = em.mark

        def tmil():
            return em.mil("tensor", sT)

        def vmil():
            return em.mil("vector", sV)

        def vedge():
            W("vector", sV, vmil())

        for dst, src in [(IDF[:], idf_d), (TRIU_I[:], triui_d),
                         (TRIU_S[:], trius_d), (TRIL_S[:], trils_d),
                         (RBIAS[:], rbias_d), (CONVW[:], convw_d),
                         (LNCOL[:], lncol_d),
                         (PVEC[:], pvec_d), (EXPECT[:], expect_d),
                         (ODDM[:], oddm_d)]:
            E("sync", lambda e, d=dst, s=src: e.dma_start(out=d, in_=s)
              .then_inc(sX.h, 16))
            sX.inc(16)
        for dst, src in [(IDB[:], idf_d), (MSK[:], msk_d),
                         (ONESC[:], onescol_d), (EBS[:], eb_d),
                         (CONVBH[:], convb_d)]:
            E("gpsimd", lambda e, d=dst, s=src: e.dma_start(out=d, in_=s)
              .then_inc(sWC.h, 16))
            sWC.inc(16)
        E("gpsimd", lambda e: e.dma_start(
            out=RPACK[:].rearrange("p (j c) -> p j c", j=NJT),
            in_=rpack_d.rearrange("j p c -> p j c")).then_inc(sWC.h, 16))
        sWC.inc(16)
        for dt in range(NJT):
            E("gpsimd", lambda e, dt=dt: e.dma_start(
                out=XBF[:, dt * TE:(dt + 1) * TE],
                in_=xt_d[dt * 128:(dt + 1) * 128, :]).then_inc(sXL.h, 16))
            sXL.inc(16)
        x_loaded = sXL.n
        consts_loaded = sX.n

        W("vector", sX, consts_loaded)
        W("vector", sWC, sWC.n)
        E("vector", lambda e: e.memset(ONES1[:], 1.0))
        E("vector", lambda e: e.memset(ZEROC[:], 0.0))
        E("vector", lambda e: e.memset(CB[:, 0:1], 1e-5))
        E("vector", lambda e: e.memset(RECV[:], 0.0))
        E("vector", lambda e: e.memset(MTS[:], 0.0))
        E("vector", lambda e: e.memset(SWB[:], 0.0))
        E("vector", lambda e: e.memset(OGS[:], 0.0))
        E("vector", lambda e: e.memset(CB[:, 1:2], 1.0))
        E("vector", lambda e: e.tensor_scalar(CONVBH[:], CONVBH[:], 0.5, None,
                                              ALU.mult))
        E("vector", lambda e: e.tensor_scalar(LNCOL[:], LNCOL[:], 0.5, None,
                                              ALU.mult))
        consts2 = vmil()
        MARK("consts", sV, consts2)

        myoff = PVEC[0:1, 1:2]
        W("gpsimd", sX, consts_loaded)
        with nc.gpsimd.register("rPar") as rPar, \
             nc.gpsimd.register("rC") as rC, \
             nc.gpsimd.register("rF") as rF, \
             nc.gpsimd.register("rEx") as rEx, \
             nc.gpsimd.register("rTgt") as rTgt, \
             nc.gpsimd.register("rCap") as rCap:

            E("gpsimd", lambda e: (e.reg_load(rPar, PVEC[0:1, 0:1]),
                                   e.reg_mov(rTgt, 16)))
            W("gpsimd", sV, consts2)
            def g_zero(e):
                with e.If_eq(rPar, 0):
                    e.dma_start(out=flagA[0:1, 0:H],
                                in_=ZEROC[0:1, 0:H].bitcast(I32)
                                ).then_inc(sEA.h, 16)
                    e.wait_ge(sEA.h, 16)
            E("gpsimd", g_zero)
            sEA.inc(16)
            zero_done = sE.n

            groups5 = [(0, 512), (512, 512), (1024, 512), (1536, 512),
                       (2048, 3)]
            W("tensor", sXL, x_loaded)
            W("tensor", sWC, sWC.n)
            W("scalar", sXL, x_loaded)
            for g0, gn in groups5:
                for dt in range(NJT):
                    E("tensor", lambda e, dt=dt, g0=g0, gn=gn: e.matmul(
                        PH[0:1, 0:gn], ONESC[:],
                        XBF[:, dt * TE + g0:dt * TE + g0 + gn],
                        start=(dt == 0), stop=(dt == NJT - 1)))
                m1 = tmil()
                W("vector", sT, m1)
                E("vector", lambda e, g0=g0, gn=gn: e.tensor_scalar(
                    MUR[0:1, g0:g0 + gn], PH[0:1, 0:gn], 1.0 / D, None,
                    ALU.mult))
                mu_ev = vmil()
                W("tensor", sV, mu_ev)
                for dt in range(NJT):
                    sl = (dt % 2) * 520
                    if dt >= 2:
                        W("scalar", sT, sT.n - 1)
                    else:
                        W("scalar", sT, sT.n)
                    E("scalar", lambda e, dt=dt, g0=g0, gn=gn, sl=sl:
                      e.activation(XSQB[:, sl:sl + gn],
                                   XBF[:, dt * TE + g0:dt * TE + g0 + gn],
                                   AF.Square).then_inc(sS.h, 1))
                    sqd = sS.inc()
                    W("tensor", sS, sqd)
                    E("tensor", lambda e, gn=gn, dt=dt, sl=sl: e.matmul(
                        PH[0:1, 0:gn], ONESC[:], XSQB[:, sl:sl + gn],
                        start=(dt == 0), stop=(dt == NJT - 1))
                      .then_inc(sT.h, 1))
                    sT.inc()
                m2 = sT.n
                W("vector", sT, m2)
                E("vector", lambda e, g0=g0, gn=gn: e.tensor_tensor(
                    ROWS[0:1, 0:gn], MUR[0:1, g0:g0 + gn],
                    MUR[0:1, g0:g0 + gn], ALU.mult))
                E("vector", lambda e, g0=g0, gn=gn: e.scalar_tensor_tensor(
                    RSR[0:1, g0:g0 + gn], PH[0:1, 0:gn], 1.0 / D,
                    ROWS[0:1, 0:gn], ALU.mult, ALU.subtract))
                var_ev = vmil()
                W("tensor", sV, var_ev)
            W("scalar", sV, sV.n)
            W("scalar", sV, consts2)
            E("scalar", lambda e: e.activation(RSR[0:1, 0:TE], RSR[0:1, 0:TE],
                                               AF.Ln, bias=CB[0:1, 0:1]))
            E("scalar", lambda e: e.activation(RSR[0:1, 0:TE], RSR[0:1, 0:TE],
                                               AF.Exp, scale=-0.5)
              .then_inc(sS.h, 1))
            rstd_done = sS.inc()
            W("tensor", sS, rstd_done)
            for g0, gn in groups5:
                E("tensor", lambda e, g0=g0, gn=gn: e.matmul(
                    PA[:, 0:gn], ONES1[:], MUR[0:1, g0:g0 + gn],
                    start=True, stop=True))
                E("tensor", lambda e, g0=g0, gn=gn: e.matmul(
                    PB[:, 0:gn], ONES1[:], RSR[0:1, g0:g0 + gn],
                    start=True, stop=True))
                bc = tmil()
                W("vector", sT, bc)
                for dt in range(NJT):
                    E("vector", lambda e, dt=dt, g0=g0, gn=gn: e.tensor_tensor(
                        XBF[:, dt * TE + g0:dt * TE + g0 + gn],
                        XBF[:, dt * TE + g0:dt * TE + g0 + gn],
                        PA[:, 0:gn], ALU.subtract))
                    E("vector", lambda e, dt=dt, g0=g0, gn=gn: e.tensor_tensor(
                        XBF[:, dt * TE + g0:dt * TE + g0 + gn],
                        XBF[:, dt * TE + g0:dt * TE + g0 + gn],
                        PB[:, 0:gn], ALU.mult))
                xn_g = vmil()
                W("tensor", sV, xn_g)
            xn_done = sV.n
            MARK("xn", sV, xn_done)

            for kname in ("wq", "wk", "wv", "wg"):
                E("gpsimd", lambda e, k=kname: e.dma_start(
                    out=WSB[k][0].rearrange("p (j c) -> p j c", j=NJT),
                    in_=w_d[k][0].rearrange("j p c -> p j c"))
                  .then_inc(sWa.h, 16))
                sWa.inc(16)
            wload = {0: (sWa, sWa.n)}

            W("tensor", sV, xn_done)
            W("tensor", sWC, sWC.n)
            W("tensor", sX, consts_loaded)
            W("scalar", sX, consts_loaded)
            def gen_SB():
                for c in range(nchunks):
                    t0 = 3 + c * CH
                    NEED("tensor", "PCb")
                    for jt in range(NJT):
                        E("tensor", lambda e, jt=jt, t0=t0: e.matmul(
                            PC[0:96, 0:CH], RPACK[:, jt * 96:(jt + 1) * 96],
                            XBF[:, jt * TE + t0:jt * TE + t0 + CH],
                            start=(jt == 0), stop=(jt == NJT - 1)))
                    zmm = tmil()
                    W("scalar", sT, zmm)
                    E("scalar", lambda e: e.activation(
                        SCR[0:96, 0:CH], PC[0:96, 0:CH], AF.Identity,
                        bias=RBIAS[:, 0:1]).then_inc(sS.h, 1))
                    zb = sS.inc()
                    MARK("PCb", sS, zb)
                    W("tensor", sS, zb)
                    E("tensor", lambda e: e.transpose(PD[0:128, 0:96],
                                                      SCR[0:96, 0:CH],
                                                      IDF[0:96, 0:96]))
                    ztr = tmil()
                    W("vector", sT, ztr)
                    E("vector", lambda e: e.tensor_copy(ZT[:, 0:96],
                                                        PD[:, 0:96]))
                    zev = vmil()
                    W("tensor", sV, zev)
                    yield

                    E("vector", lambda e: e.tensor_scalar(
                        SCR2[:, 0:32], ZT[:, 0:32], -1.0, None, ALU.mult))
                    vedge()
                    E("vector", lambda e: e.tensor_tensor(
                        SCR2[:, 0:32], SCR2[:, 0:32], ZT[:, 0:32], ALU.max))
                    absd = vmil()
                    W("scalar", sV, absd)
                    E("scalar", lambda e: e.activation(SCR2[:, 32:64],
                                                       SCR2[:, 0:32], AF.Exp,
                                                       scale=-1.0))
                    E("scalar", lambda e: e.activation(SCR2[:, 32:64],
                                                       SCR2[:, 32:64], AF.Ln,
                                                       bias=CB[:, 1:2]).then_inc(sS.h, 1))
                    lnd = sS.inc()
                    W("vector", sS, lnd)
                    E("vector", lambda e: e.tensor_scalar(
                        SCR2[:, 64:96], ZT[:, 0:32], 0.0, None, ALU.min))
                    vedge()
                    E("vector", lambda e, c=c: e.tensor_tensor(
                        LGAB[:, c * 32:(c + 1) * 32], SCR2[:, 64:96],
                        SCR2[:, 32:64], ALU.subtract))
                    lgd = vmil()
                    if c == nchunks - 1:
                        MARK("lgabF", sV, lgd)

                    W("tensor", sV, lgd)
                    NEED("tensor", "PEb")
                    E("tensor", lambda e, c=c: e.matmul(
                        PEb[:, 0:16], TRIU_S[:], LGAB[:, c * 32:c * 32 + 16],
                        start=True, stop=True))
                    E("tensor", lambda e, c=c: e.matmul(
                        PEb[:, 16:32], TRIU_I[:], LGAB[:, c * 32:c * 32 + 16],
                        start=True, stop=True))
                    E("tensor", lambda e, c=c: e.matmul(
                        PEb[:, 32:48], TRIL_S[:], LGAB[:, c * 32:c * 32 + 16],
                        start=True, stop=True))
                    csd = tmil()
                    W("vector", sT, csd)
                    E("vector", lambda e, c=c: e.tensor_tensor(
                        SCR[:, 0:16], PEb[:, 0:16],
                        LGAB[:, c * 32 + 16:c * 32 + 32], ALU.add))
                    E("vector", lambda e: e.tensor_copy(SCR[:, 16:32],
                                                        PEb[:, 16:32]))
                    E("vector", lambda e: e.tensor_copy(SCR[:, 32:48],
                                                        PEb[:, 32:48]))
                    E("vector", lambda e, c=c: e.tensor_scalar(
                        ECOL[:, c * 96 + 80:c * 96 + 96], PEb[:, 16:32], -1.0,
                        None, ALU.mult))
                    lce = vmil()
                    W("scalar", sV, lce)
                    W("tensor", sV, lce)
                    E("scalar", lambda e, c=c: e.activation(
                        ECOL[:, c * 96 + 0:c * 96 + 16], SCR[:, 0:16], AF.Exp))
                    E("scalar", lambda e, c=c: e.activation(
                        ECOL[:, c * 96 + 16:c * 96 + 32], SCR[:, 32:48], AF.Exp))
                    E("scalar", lambda e, c=c: e.activation(
                        ECOL[:, c * 96 + 32:c * 96 + 48], SCR[:, 16:32], AF.Exp))
                    E("scalar", lambda e, c=c: e.activation(
                        ECOL[:, c * 96 + 64:c * 96 + 80],
                        LGAB[:, c * 32 + 16:c * 32 + 32], AF.Exp)
                      .then_inc(sS.h, 1))
                    ecd = sS.inc()
                    W("vector", sS, ecd)
                    E("vector", lambda e, c=c: e.tensor_tensor(
                        ECOL[:, c * 96 + 48:c * 96 + 64],
                        ECOL[:, c * 96 + 16:c * 96 + 32],
                        ECOL[:, c * 96 + 32:c * 96 + 48], ALU.mult))
                    E("vector", lambda e, c=c: e.tensor_scalar(
                        ECOL[:, c * 96 + 0:c * 96 + 16],
                        ECOL[:, c * 96 + 0:c * 96 + 16], DKS, None, ALU.mult))
                    E("vector", lambda e, c=c: e.tensor_scalar(
                        ECOL[:, c * 96 + 16:c * 96 + 32],
                        ECOL[:, c * 96 + 16:c * 96 + 32], DKS, None, ALU.mult))
                    ecv = vmil()
                    yield

                    W("vector", sV, zev)
                    E("vector", lambda e: e.reduce_max(SCR2[:, 96:97],
                                                       ZT[:, 32:96], AX))
                    rmx = em.mil("vector", sV)
                    W("vector", sV, rmx)
                    E("vector", lambda e: e.tensor_scalar(
                        SCR2[:, 97:98], SCR2[:, 96:97], -1.0, None, ALU.mult)
                      .then_inc(sV.h, 1))
                    nmx = sV.inc()
                    W("scalar", sV, nmx)
                    E("scalar", lambda e: e.activation(
                        SCR[:, 128:192], ZT[:, 32:96], AF.Exp,
                        bias=SCR2[:, 97:98]).then_inc(sS.h, 1))
                    etd = sS.inc()
                    W("vector", sS, etd)
                    E("vector", lambda e: e.reduce_sum(SCR2[:, 98:99],
                                                       SCR[:, 128:192], AX))
                    rzs = em.mil("vector", sV)
                    E("vector", lambda e: e.tensor_copy(SCR[:, 192:256],
                                                        SCR[:, 128:192]))
                    vedge()
                    for it in range(7):
                        E("vector", lambda e: e.reduce_max(SCR2[:, 99:100],
                                                           SCR[:, 192:256], AX))
                        rtm = em.mil("vector", sV)
                        W("vector", sV, rtm)
                        E("vector", lambda e: e.scalar_tensor_tensor(
                            SCR[:, 192:256], SCR[:, 192:256], SCR2[:, 99:100],
                            SCR[:, 192:256], ALU.is_lt, ALU.mult))
                        vedge()
                    E("vector", lambda e: e.reduce_max(SCR2[:, 99:100],
                                                       SCR[:, 192:256], AX))
                    rt8 = em.mil("vector", sV)
                    W("vector", sV, rt8)
                    W("vector", sV, rzs)
                    E("vector", lambda e: e.reciprocal(SCR2[:, 102:103],
                                                       SCR2[:, 98:99]))
                    vedge()
                    E("vector", lambda e: e.tensor_scalar(
                        SCR2[:, 103:167], SCR[:, 128:192], SCR2[:, 99:100], None,
                        ALU.is_ge))
                    vedge()
                    E("vector", lambda e: e.tensor_tensor(
                        SCR2[:, 103:167], SCR2[:, 103:167], SCR[:, 128:192],
                        ALU.mult))
                    vedge()
                    E("vector", lambda e: e.tensor_scalar(
                        MT1[:, 0:64], SCR2[:, 103:167], SCR2[:, 102:103], None,
                        ALU.mult))
                    mbf = vmil()
                    W("tensor", sV, mbf)
                    NEED("tensor", "PFb")
                    E("tensor", lambda e: e.transpose(PF[0:64, 0:128],
                                                      MT1[:, 0:64], IDB[:]))
                    mtr = tmil()
                    W("vector", sT, mtr)
                    E("vector", lambda e, c=c: e.tensor_copy(
                        MTS[0:64, c * CH:(c + 1) * CH], PF[0:64, 0:128]))
                    mev = vmil()
                    MARK("PFb", sV, mev)
                    W("tensor", sV, mev)
                    NEED("tensor", "PH")
                    E("tensor", lambda e, c=c: e.matmul(
                        PH[0:1, 0:CH], ONESC[0:64, :],
                        MTS[0:64, c * CH:(c + 1) * CH], start=True, stop=True))
                    swm = tmil()
                    W("vector", sT, swm)
                    E("vector", lambda e: e.tensor_copy(
                        ROWS[0:1, 0:CH], PH[0:1, 0:CH]))
                    swr = vmil()
                    MARK("PH", sV, swr)
                    W("tensor", sV, swr)
                    NEED("tensor", "PEb")
                    E("tensor", lambda e: e.matmul(
                        PEb[:, 0:CH], ONES1[:], ROWS[0:1, 0:CH],
                        start=True, stop=True))
                    bsw = tmil()
                    W("vector", sT, bsw)
                    E("vector", lambda e, c=c: e.tensor_copy(
                        SWB[:, c * CH:(c + 1) * CH], PEb[:, 0:CH]))
                    sb_ev = vmil()
                    MARK("PEb", sV, sb_ev)
                    MARK("PC", sS, zb)
                    MARK("PD", sV, zev)
                    yield


            c4done_abs = 0
            spill_done = 0

            def rr(*gen_list):
                gens = [g for g in gen_list if g is not None]
                while gens:
                    alive = []
                    for g in gens:
                        try:
                            next(g)
                            alive.append(g)
                        except StopIteration:
                            pass
                    gens = alive

            rr(gen_SB())
            pending3 = None
            for h in range(nheads):
                par = h % 2
                red_mil = []
                ebc_mil = {}
                exn = {}
                erowb_ref = [0]

                def gen_P1(h=h, par=par, prev3=pending3, first=(h == 0)):
                    if h + 1 < nheads:
                        sPar = sWb if (h + 1) % 2 else sWa
                        for kname in ("wq", "wk", "wv", "wg"):
                            if h >= 1:
                                NEED("gpsimd", f"projdone{h - 1}")
                            E("gpsimd", lambda e, k=kname, hh=h + 1,
                              pp=(h + 1) % 2, s=sPar: e.dma_start(
                                out=WSB[k][pp].rearrange(
                                    "p (j c) -> p j c", j=NJT),
                                in_=w_d[k][hh].rearrange("j p c -> p j c"))
                              .then_inc(s.h, 16))
                            sPar.inc(16)
                        wload[h + 1] = (sPar, sPar.n)
                    yield

                    def erow_batch(h=h):
                        lga = LGAB[:].rearrange("p (c k) -> p c k", k=32)
                        lhs_a = lga[:, 0:nchunks, h:h + 1].rearrange(
                            "p c one -> p (c one)")
                        lhs_b = lga[:, 0:nchunks, 16 + h:17 + h].rearrange(
                            "p c one -> p (c one)")
                        NEED("tensor", "lgabF")
                        NEED("tensor", "PEb")
                        E("tensor", lambda e, lhs_a=lhs_a, nb=nchunks:
                          e.matmul(PEb[0:nb, 0:128], lhs_a, TRIU_S[:],
                                   start=True, stop=False))
                        E("tensor", lambda e, lhs_b=lhs_b, nb=nchunks:
                          e.matmul(PEb[0:nb, 0:128], lhs_b, IDF[:],
                                   start=False, stop=True))
                        E("tensor", lambda e, lhs_a=lhs_a, nb=nchunks:
                          e.matmul(PEb[0:nb, 128:256], lhs_a, TRIU_I[:],
                                   start=True, stop=True))
                        ermb = tmil()
                        W("scalar", sT, ermb)
                        W("scalar", sE, sE.n)
                        W("scalar", sE2, sE2.n)
                        E("scalar", lambda e, nb=nchunks: e.activation(
                            EROWB[0:nb, 0:256], PEb[0:nb, 0:256],
                            AF.Identity).then_inc(sS.h, 1))
                        erowb_ref[0] = sS.inc()
                        MARK("PEb", sS, erowb_ref[0])

                    if not first:
                        erow_batch()
                        yield

                    W("tensor", wload[h][0], wload[h][1])
                    prj = [("wq", QPRE, False), ("wk", KPRE, False),
                           ("wv", VPRE, False), ("wg", None, True)]
                    groups5 = [(0, 512), (512, 512), (1024, 512),
                               (1536, 512), (2048, 3)]
                    ggroups = [(3, 512), (515, 512), (1027, 512), (1539, 512)]
                    pp = 0
                    for kname, dstpre, is_g in prj:
                        if is_g and prev3 is not None:
                            for _ in prev3:
                                pass
                        if is_g and first:
                            erow_batch()
                        gs = ggroups if is_g else groups5
                        for gi, (g0, gn) in enumerate(gs):
                            PSU = PA if pp % 2 == 0 else PB
                            pp += 1
                            NEED("tensor", "PA" if PSU is PA else "PB")
                            for jt in range(NJT):
                                E("tensor", lambda e, jt=jt, g0=g0, gn=gn,
                                  P=PSU, k=kname, par=par: e.matmul(
                                    P[:, 0:gn],
                                    WSB[k][par][:, jt * 128:(jt + 1) * 128],
                                    XBF[:, jt * TE + g0:jt * TE + g0 + gn],
                                    start=(jt == 0), stop=(jt == NJT - 1)))
                            pm = tmil()
                            if is_g:
                                W("scalar", sT, pm)
                                NEED("scalar", "GTHfree")
                                E("scalar", lambda e, g0=g0, gn=gn, P=PSU:
                                  e.activation(GTH[:, g0 - 3:g0 - 3 + gn],
                                               P[:, 0:gn], AF.Tanh, scale=0.5)
                                  .then_inc(sS.h, 1))
                                ev = sS.inc()
                                MARK("PA" if PSU is PA else "PB", sS, ev)
                            else:
                                W("vector", sT, pm)
                                E("vector", lambda e, g0=g0, gn=gn, P=PSU,
                                  d=dstpre: e.tensor_copy(d[:, g0:g0 + gn],
                                                          P[:, 0:gn]))
                                ev = vmil()
                                MARK("PA" if PSU is PA else "PB", sV, ev)
                            yield
                    MARK(f"projdone{h}", sV, sV.n)
                    NEED("vector", "consts")
                    for tn in range(3):
                        for tau in range(4):
                            E("vector", lambda e, tn=tn, tau=tau, h=h:
                              e.tensor_scalar(
                                DIAG[:, (tn * 4 + tau) * 128:
                                     (tn * 4 + tau + 1) * 128],
                                IDB[:], CONVW[:, h * 12 + tn * 4 + tau:
                                              h * 12 + tn * 4 + tau + 1],
                                0.5, ALU.mult, ALU.mult))
                    diagd = vmil()
                    W("tensor", sV, diagd)
                    yield

                    pres = [QPRE, KPRE, VPRE]
                    outs = [QT, KT, VT]
                    pp = 0
                    for gi, (g0o, gn) in enumerate(ggroups):
                        for tn in range(3):
                            PSU = PA if pp % 2 == 0 else PB
                            TH = THBD[pp % 2]
                            pp += 1
                            NEED("tensor", "PA" if PSU is PA else "PB")
                            for tau in range(4):
                                E("tensor", lambda e, tn=tn, tau=tau, g0o=g0o,
                                  gn=gn, P=PSU: e.matmul(
                                    P[:, 0:gn],
                                    DIAG[:, (tn * 4 + tau) * 128:
                                         (tn * 4 + tau + 1) * 128],
                                    pres[tn][:, g0o - 3 + tau:
                                             g0o - 3 + tau + gn],
                                    start=(tau == 0), stop=(tau == 3)))
                            cm = tmil()
                            W("scalar", sT, cm)
                            NEED("scalar", "THBw%d" % ((pp - 1) % 2))
                            E("scalar", lambda e, tn=tn, gn=gn, P=PSU, h=h,
                              TH=TH: e.activation(TH[0:128, 0:gn], P[:, 0:gn],
                                                  AF.Tanh,
                                                  bias=CONVBH[:, h * 3 + tn:
                                                              h * 3 + tn + 1])
                              .then_inc(sS.h, 1))
                            thd = sS.inc()
                            W("vector", sS, thd)
                            E("vector", lambda e, gn=gn, TH=TH:
                              e.tensor_scalar(
                                TH[:, 0:gn], TH[:, 0:gn], 1.0, None,
                                ALU.add))
                            E("vector", lambda e, tn=tn, g0o=g0o, gn=gn,
                              P=PSU, h=h, TH=TH: e.scalar_tensor_tensor(
                                outs[tn][:, g0o - 3:g0o - 3 + gn], P[:, 0:gn],
                                CONVBH[:, h * 3 + tn:h * 3 + tn + 1],
                                TH[:, 0:gn], ALU.add, ALU.mult))
                            ev = vmil()
                            MARK("PA" if PSU is PA else "PB", sV, ev)
                            MARK("THBw%d" % ((pp - 1) % 2), sV, ev)
                        MARK(f"cv{h}_{gi}", sV, sV.n)
                        if gi == 0:
                            yield "chunks_ok"
                        else:
                            yield

                g1 = gen_P1()
                p3 = pending3
                pending3 = None
                while g1 is not None:
                    stop_here = False
                    for g in (p3, g1):
                        if g is None:
                            continue
                        try:
                            r = next(g)
                            if g is g1 and r == "chunks_ok":
                                stop_here = True
                        except StopIteration:
                            if g is p3:
                                p3 = None
                            else:
                                g1 = None
                    if stop_here:
                        break
                carry = [g for g in (p3, g1) if g is not None]

                rtgt_recv = sR.inc(16)
                def g_recv(e, h=h, rt=rtgt_recv):
                    with e.If_eq(rPar, 1):
                        e.reg_load(rEx, EXPECT[0:1, h:h + 1])
                        e.reg_mov(rC, 1)
                        e.reg_mov(rCap, 500000)
                        with e.While(rC):
                            e.dma_start(out=FLG[0:1, 0:1],
                                        in_=flagA[0:1, h:h + 1]
                                        ).then_inc(sPoll, 16)
                            e.wait_ge(sPoll, rTgt)
                            e.reg_add(rTgt, rTgt, 16)
                            e.reg_load(rF, FLG[0:1, 0:1])
                            e.reg_sub(rC, rF, rEx)
                            e.reg_sub(rCap, rCap, 1)
                            with e.If_eq(rCap, 0):
                                e.reg_mov(rC, 0)
                        e.dma_start(out=RECV[:],
                                    in_=xchgA[h * 128:(h + 1) * 128, :]
                                    ).then_inc(sR.h, 16)
                        e.wait_ge(sR.h, rt)
                E("gpsimd", g_recv)
                E("gpsimd", lambda e: e.engine_nop().then_inc(sG.h, 1))
                recv_mil = sG.inc()
                W("vector", sG, recv_mil)
                E("vector", lambda e: e.tensor_scalar(
                    ST[:].bitcast(I32), RECV[:].bitcast(I32),
                    ODDM[:, 0:1], None, ALU.bitwise_and))
                E("vector", lambda e: e.tensor_copy(STB[:], ST[:]))
                st_init = vmil()
                MARK("STB", sV, st_init)
                MARK("ST", sV, st_init)

                def gen_A(c):
                    sl = c % NSLOT
                    ec = lambda k, c=c, hh=h: ECOL[:, c * 96 + k * 16 + hh:
                                                   c * 96 + k * 16 + hh + 1]
                    NEED("tensor", f"c4t_{h}_{c - NSLOT}",
                         f"c4v_{h}_{c - NSLOT}")
                    NEED("tensor", f"cv{h}_{c // 4}")
                    NEED("tensor", "PC")
                    E("tensor", lambda e, c=c: e.matmul(
                        PC[:, 0:128], KT[:, c * CH:(c + 1) * CH],
                        KT[:, c * CH:(c + 1) * CH], start=True, stop=True))
                    E("tensor", lambda e, c=c: e.matmul(
                        PC[:, 128:256], KT[:, c * CH:(c + 1) * CH],
                        QT[:, c * CH:(c + 1) * CH], start=True, stop=True))
                    gkq = tmil()
                    W("vector", sT, gkq)
                    E("vector", lambda e, sl=sl: e.tensor_tensor(
                        APK[sl][:, 0:256], PC[:, 0:256], MSK[:], ALU.mult))
                    gev = vmil()
                    MARK("PC", sV, gev)
                    sEx = sE if c % 2 == 0 else sE2
                    if c >= 2:
                        W("sync", sT, ebc_mil[c - 2])
                    W("sync", sS, erowb_ref[0])
                    E("sync", lambda e, c=c, sEx=sEx: e.dma_start(
                        out=EROW2[0:1, (c % 2) * 256:(c % 2) * 256 + 256],
                        in_=EROWB[c:c + 1, 0:256]).then_inc(sEx.h, 16))
                    exn[c] = (sEx, sEx.inc(16))
                    yield
                    W("tensor", *exn[c])
                    NEED("tensor", "PEb")
                    E("tensor", lambda e, c=c: e.matmul(
                        PEb[:, 0:256], ONES1[:],
                        EROW2[0:1, (c % 2) * 256:(c % 2) * 256 + 256],
                        start=True, stop=True))
                    ebc = tmil()
                    ebc_mil[c] = ebc
                    yield
                    W("vector", sT, ebc)
                    NEED("vector", "ECLs")
                    E("vector", lambda e, c=c, ec=ec: e.tensor_scalar(
                        ECL[:, 0:256], PEb[:, 0:256], ec(5), 0.0,
                        ALU.add, ALU.min))
                    ecl = vmil()
                    MARK("PEb", sV, ecl)
                    W("scalar", sV, ecl)
                    NEED("scalar", "ETf")
                    E("scalar", lambda e: e.activation(
                        ET[:, 0:256], ECL[:, 0:256], AF.Exp).then_inc(sS.h, 1))
                    etile = sS.inc()
                    MARK("ECLs", sS, etile)
                    yield
                    W("gpsimd", sV, gev)
                    W("gpsimd", sS, etile)
                    E("gpsimd", lambda e, sl=sl: e.tensor_tensor(
                        APK[sl][:, 0:256], APK[sl][:, 0:256], ET[:, 0:256],
                        ALU.mult).then_inc(sG.h, 1))
                    apk_done = sG.inc()
                    MARK(f"apk{sl}", sG, apk_done)
                    MARK("ETf", sG, apk_done)
                    MARK("ECLf", sG, apk_done)
                    yield
                    NEED("tensor", "PF")
                    E("tensor", lambda e, c=c: e.transpose(
                        PF[:, 0:128], KT[:, c * CH:(c + 1) * CH], IDB[:]))
                    t1 = tmil()
                    W("scalar", sT, t1)
                    E("scalar", lambda e, sl=sl, c=c, ec=ec: e.activation(
                        RS[sl][:, 0:128], PF[:, 0:128], AF.Identity,
                        scale=ec(0)))
                    E("scalar", lambda e, sl=sl, c=c, ec=ec: e.activation(
                        GKS[sl][:], PF[:, 0:128], AF.Identity,
                        scale=ec(1)).then_inc(sS.h, 1))
                    t1e = sS.inc()
                    MARK("PF", sS, t1e)
                    yield
                    NEED("tensor", "PF")
                    E("tensor", lambda e, c=c: e.transpose(
                        PF[:, 0:128], VT[:, c * CH:(c + 1) * CH], IDB[:]))
                    t2 = tmil()
                    W("scalar", sT, t2)
                    E("scalar", lambda e, sl=sl, c=c, ec=ec: e.activation(
                        RS[sl][:, 128:256], PF[:, 0:128], AF.Identity,
                        scale=ec(4)).then_inc(sS.h, 1))
                    t2e = sS.inc()
                    MARK("PF", sS, t2e)
                    MARK(f"rs{sl}", sS, t2e)

                def gen_B(c):
                    sl = c % NSLOT
                    NEED("tensor", f"apk{sl}", f"rs{sl}")
                    for j in range(J_NEUMANN):
                        src = RS[sl] if j == 0 else XS[(j - 1) % 2]
                        PSOL = PC if j % 2 == 0 else PD
                        NEED("tensor", "PC" if j % 2 == 0 else "PD")
                        E("tensor", lambda e, sl=sl, s=src, P=PSOL: e.matmul(
                            P[:, 0:256], APK[sl][:, 0:128], s[:, 0:256],
                            start=True, stop=True))
                        sm = tmil()
                        W("vector", sT, sm)
                        E("vector", lambda e, sl=sl, j=j, P=PSOL:
                          e.tensor_tensor(XS[j % 2][:, 0:256], P[:, 0:256],
                                          RS[sl][:, 0:256], ALU.add))
                        xev = vmil()
                        MARK("PC" if j % 2 == 0 else "PD", sV, xev)
                        W("tensor", sV, xev)
                        yield
                    xf = XS[(J_NEUMANN - 1) % 2]
                    NEED("tensor", "PF")
                    E("tensor", lambda e, x=xf: e.transpose(
                        PF[:, 0:128], x[:, 0:128], IDB[:]))
                    t3 = tmil()
                    W("vector", sT, t3)
                    E("vector", lambda e, sl=sl: e.tensor_copy(WTS[sl][:],
                                                               PF[:, 0:128]))
                    E("vector", lambda e, sl=sl, x=xf: e.tensor_copy(
                        U0S[sl][:], x[:, 128:256]))
                    t3e = vmil()
                    MARK("PF", sV, t3e)
                    MARK(f"set{sl}", sV, t3e)
                    E("vector", lambda e, sl=sl: e.tensor_copy(
                        PBT[sl][:], APK[sl][:, 128:256]))
                    vmil()

                def gen_C(c):
                    nonlocal c4done_abs
                    sl = c % NSLOT
                    ec = lambda k, c=c, hh=h: ECOL[:, c * 96 + k * 16 + hh:
                                                   c * 96 + k * 16 + hh + 1]
                    NEED("tensor", f"set{sl}", "STB")
                    NEED("tensor", "PG")
                    E("tensor", lambda e, sl=sl: e.matmul(
                        PG[:], WTS[sl][:], STB[:], start=True, stop=True))
                    um = tmil()
                    W("vector", sT, um)
                    E("vector", lambda e, sl=sl: e.tensor_tensor(
                        UB[:], PG[:], U0S[sl][:], ALU.subtract))
                    uev = vmil()
                    MARK("PG", sV, uev)
                    MARK("UB", sV, uev)
                    yield
                    NEED("tensor", "PA")
                    E("tensor", lambda e, c=c: e.matmul(
                        PA[:, 0:128], QT[:, c * CH:(c + 1) * CH], STB[:],
                        start=True, stop=True))
                    o1 = tmil()
                    NEED("tensor", "UB", "PB")
                    E("tensor", lambda e, sl=sl: e.matmul(
                        PB[:, 0:128], PBT[sl][:], UB[:], start=True,
                        stop=True))
                    o2 = tmil()
                    yield
                    NEED("tensor", "PG")
                    E("tensor", lambda e, sl=sl: e.matmul(
                        PG[:], GKS[sl][:], UB[:], start=True, stop=True))
                    fm = tmil()
                    W("scalar", sT, o1)
                    E("scalar", lambda e, c=c, ec=ec: e.activation(
                        OB[:, c * 128:(c + 1) * 128], PA[:, 0:128],
                        AF.Identity, scale=ec(2)).then_inc(sS.h, 1))
                    o1e = sS.inc()
                    MARK("PA", sS, o1e)
                    W("vector", sS, o1e)
                    W("vector", sT, o2)
                    E("vector", lambda e, c=c: e.tensor_tensor(
                        OB[:, c * 128:(c + 1) * 128],
                        OB[:, c * 128:(c + 1) * 128], PB[:, 0:128], ALU.add))
                    oev = vmil()
                    MARK("PB", sV, oev)
                    yield
                    W("vector", sT, fm)
                    E("vector", lambda e, c=c, ec=ec: e.scalar_tensor_tensor(
                        ST[:], ST[:], ec(3), PG[:], ALU.mult,
                        ALU.subtract))
                    E("vector", lambda e: e.tensor_copy(STB[:], ST[:]))
                    stev = vmil()
                    MARK("PG", sV, stev)
                    MARK("STB", sV, stev)
                    MARK(f"c4t_{h}_{c}", sT, fm)
                    MARK(f"c4v_{h}_{c}", sV, stev)
                    if c == nchunks - 1:
                        MARK(f"fold_{h}", sV, stev)
                    yield
                    W("scalar", sV, oev)
                    if red_mil:
                        W("scalar", sV, red_mil[-1])
                    E("scalar", lambda e, c=c: e.activation(
                        OT[:, 0:128], OB[:, c * 128:(c + 1) * 128], AF.Square)
                      .then_inc(sS.h, 1))
                    sqd = sS.inc()
                    E("vector", lambda e, c=c: e.tensor_reduce(
                        VARB[:, c:c + 1], OB[:, c * 128:(c + 1) * 128],
                        AX, ALU.add))
                    W("vector", sS, sqd)
                    E("vector", lambda e, c=c: e.tensor_reduce(
                        VARB[:, 16 + c:17 + c], OT[:, 0:128], AX, ALU.add))
                    red_mil.append(vmil())
                    if debug and c < 4:
                        E("vector", lambda e, c=c: e.tensor_copy(
                            DBGB[:, c * 272:c * 272 + 128],
                            OB[:, c * 128:(c + 1) * 128]))
                    c4done_abs += 1

                for i in range(nchunks + 2):
                    gens = []
                    if i < nchunks:
                        gens.append(gen_A(i))
                    if i >= 2:
                        gens.append(gen_C(i - 2))
                    if 1 <= i <= nchunks:
                        gens.append(gen_B(i - 1))
                    gens.extend(carry)
                    carry = []
                    while gens:
                        alive = []
                        for g in gens:
                            try:
                                next(g)
                                alive.append(g)
                            except StopIteration:
                                pass
                        gens = alive

                def gen_P3(h=h, red_mil=red_mil):
                    NEED("gpsimd", f"fold_{h}")
                    t1v = sEA.inc(16)
                    t2v = sEA.inc(16)
                    def g_pub(e, h=h, t1v=t1v, t2v=t2v):
                        with e.If_eq(rPar, 0):
                            e.dma_start(out=xchgA[h * 128:(h + 1) * 128, :],
                                        in_=ST[:]).then_inc(sEA.h, 16)
                            e.wait_ge(sEA.h, t1v)
                            e.dma_start(out=flagA[0:1, h:h + 1],
                                        in_=EXPECT[0:1, h:h + 1]
                                        ).then_inc(sEA.h, 16)
                            e.wait_ge(sEA.h, t2v)
                    E("gpsimd", g_pub)
                    E("gpsimd", lambda e: e.engine_nop().then_inc(sG.h, 1))
                    pubm = sG.inc()
                    MARK("pub", sG, pubm)
                    W("vector", sG, pubm)
                    yield

                    W("vector", sV, red_mil[-1])
                    vb = lambda a, nb=nchunks: VARB[:, a:a + nb]
                    vbi = lambda a, nb=nchunks: VARB[:, a:a + nb].bitcast(I32)
                    E("vector", lambda e, vb=vb: e.tensor_scalar(
                        vb(32), vb(0), 1.0 / 128, None, ALU.mult))
                    vedge()
                    E("vector", lambda e, vb=vb: e.tensor_tensor(
                        vb(48), vb(32), vb(32), ALU.mult))
                    vedge()
                    E("vector", lambda e, vb=vb: e.scalar_tensor_tensor(
                        vb(64), vb(16), 1.0 / 128, vb(48),
                        ALU.mult, ALU.subtract))
                    vedge()
                    E("vector", lambda e, vb=vb: e.tensor_scalar(
                        vb(64), vb(64), 1e-5, None, ALU.add))
                    vedge()
                    E("vector", lambda e, vb=vb, vbi=vbi: e.tensor_copy(
                        vb(80), vbi(64)))
                    vedge()
                    E("vector", lambda e, vb=vb: e.tensor_scalar(
                        vb(80), vb(80), -0.5, QUAKE_C, ALU.mult, ALU.add))
                    vedge()
                    E("vector", lambda e, vb=vb, vbi=vbi: e.tensor_copy(
                        vbi(96), vb(80)))
                    vedge()
                    for _ in range(3):
                        E("vector", lambda e, vb=vb: e.tensor_tensor(
                            vb(80), vb(96), vb(96), ALU.mult))
                        vedge()
                        E("vector", lambda e, vb=vb: e.tensor_tensor(
                            vb(80), vb(80), vb(64), ALU.mult))
                        vedge()
                        E("vector", lambda e, vb=vb: e.tensor_scalar(
                            vb(80), vb(80), -0.5, 1.5, ALU.mult, ALU.add))
                        vedge()
                        E("vector", lambda e, vb=vb: e.tensor_tensor(
                            vb(96), vb(96), vb(80), ALU.mult))
                        vedge()
                    E("vector", lambda e, vb=vb: e.scalar_tensor_tensor(
                        vb(112), vb(32), -1.0, vb(96),
                        ALU.mult, ALU.mult))
                    chain_done = vmil()
                    W("scalar", sV, chain_done)
                    yield

                    t4m, gm = {}, {}
                    last_am = 0
                    for c in range(nchunks):
                        half = c % 2
                        if c >= 2:
                            W("scalar", sT, t4m[c - 2])
                        else:
                            NEED("scalar", "T4last")
                        E("scalar", lambda e, c=c, half=half: e.activation(
                            OLBD[:, half * 128:half * 128 + 128],
                            OB[:, c * 128:(c + 1) * 128], AF.Identity,
                            scale=VARB[:, 96 + c:97 + c],
                            bias=VARB[:, 112 + c:113 + c]).then_inc(sS.h, 1))
                        last_am = sS.inc()
                        PFX = PF[:, 0:128]
                        W("tensor", sS, last_am)
                        if c >= 1:
                            W("tensor", sV, gm[c - 1])
                        else:
                            NEED("tensor", "PF")
                        E("tensor", lambda e, half=half, P=PFX: e.transpose(
                            P, OLBD[:, half * 128:half * 128 + 128], IDB[:]))
                        t4m[c] = tmil()
                        W("vector", sT, t4m[c])
                        E("vector", lambda e, P=PFX: e.tensor_scalar(
                            OT[:, 128:256], P, LNCOL[:, 0:1], LNCOL[:, 1:2],
                            ALU.mult, ALU.add))
                        E("vector", lambda e, c=c: e.scalar_tensor_tensor(
                            OGS[:, c * CH:(c + 1) * CH],
                            GTH[:, c * CH:(c + 1) * CH], 1.0, OT[:, 128:256],
                            ALU.add, ALU.mult))
                        gm[c] = vmil()
                        if debug and c < 4:
                            E("vector", lambda e, c=c, half=half: e.tensor_copy(
                                DBGB[:, c * 272 + 128:c * 272 + 256],
                                OLBD[:, half * 128:half * 128 + 128]))
                            E("vector", lambda e, c=c: e.tensor_copy(
                                DBGB[:, c * 272 + 256:c * 272 + 257],
                                VARB[:, 96 + c:97 + c]))
                            E("vector", lambda e, c=c: e.tensor_copy(
                                DBGB[:, c * 272 + 257:c * 272 + 258],
                                VARB[:, 112 + c:113 + c]))
                        yield
                    MARK("PF", sV, gm[nchunks - 1])
                    MARK("GTHfree", sV, gm[nchunks - 1])
                    MARK("T4last", sT, t4m[nchunks - 1])

                    W("gpsimd", sV, gm[nchunks - 1])
                    E("gpsimd", lambda e, h=h: e.dma_start(
                        out=og_d[h * 128:(h + 1) * 128, :], in_=OGS[:])
                      .then_inc(sP.h, 16))
                    sP.inc(16)
                    MARK("spill", sP, sP.n)
                    NEED("vector", "spill")

                pending3 = gen_P3()

            rr(pending3)
            pending3 = None

            allspill = sP.n
            c_all_v, c_all_s, c_all_t, c_all_g = sV.n, sS.n, sT.n, sG.n
            for eng in ("tensor", "vector", "scalar", "gpsimd", "sync"):
                W(eng, sV, c_all_v)
                W(eng, sS, c_all_s)
                W(eng, sT, c_all_t)
                W(eng, sG, c_all_g)
                W(eng, sP, allspill)

            W("tensor", sWC, sWC.n)
            if not no_stage_d:
                W("sync", sT, sT.n)
                for ht in range(16):
                    E("sync", lambda e, ht=ht: e.dma_start(
                        out=OGT[:, ht * 2048:(ht + 1) * 2048],
                        in_=og_d[ht * 128:(ht + 1) * 128, :])
                      .then_inc(sOG.h, 16))
                    sOG.inc(16)
                og_ld = sOG.n
                W("tensor", sOG, og_ld)
            for mt in range(16 if not no_stage_d else 0):
                pi = mt % 2
                NEED("gpsimd", f"wos{pi}")
                sPar = sWa if pi == 0 else sWb
                E("gpsimd", lambda e, mt=mt, pi=pi, s=sPar: e.dma_start(
                    out=WOD[pi].rearrange("p (j c) -> p j c", j=16),
                    in_=wo_d[mt].rearrange("j p c -> p j c"))
                  .then_inc(s.h, 16))
                sPar.inc(16)
                W("tensor", sPar, sPar.n)
                for tq in range(4):
                    qi = (mt * 4 + tq) % 2
                    PSU = PA if qi == 0 else PB
                    NEED("tensor", "PA" if qi == 0 else "PB")
                    for ht in range(16):
                        E("tensor", lambda e, ht=ht, pi=pi, tq=tq, P=PSU:
                          e.matmul(
                            P[:], WOD[pi][:, ht * 128:(ht + 1) * 128],
                            OGT[:, ht * 2048 + tq * 512:
                                 ht * 2048 + (tq + 1) * 512],
                            start=(ht == 0), stop=(ht == 15)))
                    wom = tmil()
                    if tq == 3:
                        MARK(f"wos{pi}", sT, wom)
                    NEED("tensor", "PC", "PD")
                    E("tensor", lambda e, mt=mt, tq=tq: e.matmul(
                        PC[:, 0:256], EBS[:, mt * 128:(mt + 1) * 128],
                        MTS[:, tq * 512:tq * 512 + 256], start=True,
                        stop=True))
                    E("tensor", lambda e, mt=mt, tq=tq: e.matmul(
                        PD[:, 0:256], EBS[:, mt * 128:(mt + 1) * 128],
                        MTS[:, tq * 512 + 256:tq * 512 + 512], start=True,
                        stop=True))
                    ebm = tmil()
                    bi = (mt * 4 + tq) % 2
                    NEED("sync", f"xr{bi}")
                    E("sync", lambda e, mt=mt, tq=tq, bi=bi: e.dma_start(
                        out=XRT[bi][:],
                        in_=xt_d[mt * 128:(mt + 1) * 128,
                                 3 + tq * 512:3 + (tq + 1) * 512])
                      .then_inc(sXR.h, 16))
                    sXR.inc(16)
                    xr_ld = sXR.n
                    W("gpsimd", sV, sV.n)
                    E("gpsimd", lambda e, mt=mt, tq=tq, bi=bi: e.tensor_tensor(
                        FFN[bi][:], XBF[:, mt * TE + 3 + tq * 512:
                                        mt * TE + 3 + (tq + 1) * 512],
                        SWB[:, tq * 512:(tq + 1) * 512], ALU.mult)
                      .then_inc(sG.h, 1))
                    ffg = sG.inc()
                    W("vector", sG, ffg)
                    W("vector", sT, ebm)
                    E("vector", lambda e, bi=bi: e.tensor_tensor(
                        FFN[bi][:, 0:256], FFN[bi][:, 0:256], PC[:, 0:256],
                        ALU.add))
                    E("vector", lambda e, bi=bi: e.tensor_tensor(
                        FFN[bi][:, 256:512], FFN[bi][:, 256:512],
                        PD[:, 0:256], ALU.add))
                    ffv = vmil()
                    MARK("PC", sV, ffv)
                    MARK("PD", sV, ffv)
                    W("scalar", sV, ffv)
                    E("scalar", lambda e, bi=bi: e.activation(
                        THD[:], FFN[bi][:], AF.Tanh, scale=0.5)
                      .then_inc(sS.h, 1))
                    thf = sS.inc()
                    W("vector", sS, thf)
                    E("vector", lambda e: e.tensor_scalar(
                        THD[:], THD[:], 0.5, 0.5, ALU.mult, ALU.add))
                    E("vector", lambda e, bi=bi: e.tensor_tensor(
                        FFN[bi][:], FFN[bi][:], THD[:], ALU.mult))
                    W("vector", sT, wom)
                    W("vector", sXR, xr_ld)
                    W("vector", sO, sO.n)
                    E("vector", lambda e, bi=bi, P=PSU: e.tensor_tensor(
                        OUTS[bi][:], FFN[bi][:], P[:], ALU.mult))
                    E("vector", lambda e, bi=bi: e.tensor_tensor(
                        OUTS[bi][:], OUTS[bi][:], XRT[bi][:], ALU.add))
                    outv = vmil()
                    MARK("PA" if qi == 0 else "PB", sV, outv)
                    MARK(f"xr{bi}", sV, outv)
                    W("sync", sV, outv)
                    E("sync", lambda e, mt=mt, tq=tq, bi=bi: e.dma_start(
                        out=out_d[mt * 128:(mt + 1) * 128,
                                  tq * 512:(tq + 1) * 512],
                        in_=OUTS[bi][:]).then_inc(sO.h, 16))
                    sO.inc(16)
                    MARK(f"outdma{bi}", sO, sO.n)

            if debug:
                ddm = {"XBF": XBF, "MTS": MTS, "SWB": SWB, "ECOL": ECOL,
                       "LGAB": LGAB, "ST": ST, "MUR": MUR, "RSR": RSR,
                       "DBGB": DBGB, "KT": KT, "QT": QT, "VT": VT,
                       "GTH": GTH, "OGS": OGS, "UB": UB,
                       "WTS0": WTS[0], "U0S0": U0S[0], "PBT0": PBT[0],
                       "GKS0": GKS[0], "APK0": APK[0], "RS0": RS[0],
                       "KPRE": KPRE}
                for eng in ("sync",):
                    W(eng, sV, sV.n); W(eng, sS, sS.n)
                    W(eng, sT, sT.n); W(eng, sG, sG.n)
                for name, shape, dt in debug:
                    if name == "og":
                        E("sync", lambda e, d=dbg[name]: e.dma_start(
                            out=d, in_=og_d).then_inc(sO.h, 16))
                    else:
                        E("sync", lambda e, d=dbg[name], s=ddm[name]:
                          e.dma_start(out=d, in_=s[:]).then_inc(sO.h, 16))
                    sO.inc(16)
            W("sync", sO, sO.n)

            TENSORS.clear()
            TENSORS.update(dict(XBF=XBF, KT=KT, QT=QT, VT=VT, GTH=GTH,
                                OGS=OGS, MTS=MTS, SWB=SWB, LGAB=LGAB,
                                ECOL=ECOL, ST=ST, STB=STB, MUR=MUR, RSR=RSR,
                                KPRE=KPRE, QPRE=QPRE, VPRE=VPRE, UB=UB,
                                OT=OT, ET=ET, RECV=RECV, DIAG=DIAG,
                                PBT0=PBT[0], U0S0=U0S[0], WTS0=WTS[0],
                                GKS0=GKS[0], XS0=XS[0], XS1=XS[1],
                                RS0=RS[0], APK0=APK[0], EROWB=EROWB,
                                VARB=VARB, OLBD=OLBD, OB=OB, IDB=IDB,
                                CONVW=CONVW,
                                WSBq0=WSB["wq"][0], WSBk0=WSB["wk"][0],
                                WSBk1=WSB["wk"][1], WSBg1=WSB["wg"][1],
                                THB=THB, IDF=IDF))
            block = st.enter_context(nc.Block())

            @block.sync
            def _(eng):
                for f in em.q["sync"]:
                    f(eng)

            @block.tensor
            def _(eng):
                for f in em.q["tensor"]:
                    f(eng)

            @block.vector
            def _(eng):
                for f in em.q["vector"]:
                    f(eng)

            @block.scalar
            def _(eng):
                for f in em.q["scalar"]:
                    f(eng)

            @block.gpsimd
            def _(eng):
                for f in em.q["gpsimd"]:
                    f(eng)

    return nc


def kernel(**inputs):
    in_maps = _prep_inputs(inputs)
    nc = build_program()
    res = run_bass_kernel_spmd(nc, in_maps, core_ids=list(range(8)))
    out = np.zeros((4, 4096, 2048), np.float32)
    for b in range(4):
        for half in range(2):
            o = res.results[b * 2 + half]["outT"]
            out[b, half * TC:(half + 1) * TC, :] = o.T
    return out


# version 21 (final)
# speedup vs baseline: 1.0469x; 1.0469x over previous
import contextlib
import numpy as np
import concourse.bass as bass
import concourse.mybir as mybir
from concourse.bass_utils import run_bass_kernel_spmd

F32 = mybir.dt.float32
BF16 = mybir.dt.bfloat16
I32 = mybir.dt.int32
AF = mybir.ActivationFunctionType
ALU = mybir.AluOpType
AX = mybir.AxisListType.X

H = 16
D = 2048
TC = 2048
TE = TC + 3
CH = 128
NCH = TC // CH
NJT = D // 128
J_NEUMANN = 4
QUAKE_C = 1597463007.0
DKS = float(128) ** -0.5


def _prep_inputs(inp):
    f32 = np.float32
    C = np.ascontiguousarray
    x = inp["L0_attn_x"]

    def strips(W):
        return C(W.reshape(NJT, 128, H, 128).transpose(2, 0, 1, 3))

    qcw, kcw, vcw = inp["L0_attn_qcw"], inp["L0_attn_kcw"], inp["L0_attn_vcw"]
    qcb, kcb, vcb = inp["L0_attn_qcb"], inp["L0_attn_kcb"], inp["L0_attn_vcb"]
    shared = {
        "wq": strips(inp["L0_attn_Wq"]), "wk": strips(inp["L0_attn_Wk"]),
        "wv": strips(inp["L0_attn_Wv"]), "wg": strips(inp["L0_attn_Wg"]),
        "wo": C(inp["L0_attn_Wo"].reshape(16, 128, 16, 128)
                .transpose(2, 0, 1, 3)),
        "rpack": C(np.concatenate([inp["L0_attn_Wa"], inp["L0_attn_Wb"],
                                   inp["L0_ffn_router_weight"].T], axis=1)
                   .reshape(NJT, 128, 96)),
        "rbias": C(np.concatenate([inp["L0_attn_ba"], inp["L0_attn_bb"],
                                   inp["L0_ffn_router_bias"]])
                   .reshape(96, 1).astype(f32)),
        "eb": C(inp["L0_ffn_expert_bias"].astype(f32)),
        "convw": C(np.concatenate(
            [w.reshape(H, 128, 1, 4) for w in (qcw, kcw, vcw)], axis=2)
            .transpose(1, 0, 2, 3).reshape(128, H * 12)),
        "convb": C(np.concatenate(
            [b.reshape(H, 128, 1) for b in (qcb, kcb, vcb)], axis=2)
            .transpose(1, 0, 2).reshape(128, H * 3)),
        "lncol": C(np.stack([inp["L0_attn_ln_w"].reshape(128),
                             inp["L0_attn_ln_b"].reshape(128)], axis=1)),
        "idf": np.eye(128, dtype=f32),
        "triu_i": C(np.triu(np.ones((128, 128), f32), 0)),
        "triu_s": C(np.triu(np.ones((128, 128), f32), 1)),
        "tril_s": C(np.tril(np.ones((128, 128), f32), -1)),
        "msk": C(np.concatenate(
            [-np.triu(np.ones((128, 128), f32), 1) / 128.0,
             -np.triu(np.ones((128, 128), f32), 0) * DKS], axis=1)),
        "ones1": np.ones((1, 128), f32),
        "onescol": np.ones((128, 1), f32),
    }
    rng = np.random.default_rng()
    nonce = int(rng.integers(1 << 16, 1 << 29))
    expect = (nonce + np.arange(H)).astype(np.int32).reshape(1, H)
    in_maps = []
    for b in range(4):
        for half in range(2):
            xt = np.zeros((D, TE), f32)
            if half == 0:
                xt[:, 3:] = x[b, 0:TC, :].T
            else:
                xt[:, :] = x[b, TC - 3:2 * TC, :].T
            pv = np.array([[half, half * (H * 128 * 128),
                            (1 - half) * (H * 128 * 128),
                            half * H, (1 - half) * H, 0, 0, 0]], np.int32)
            m = {"xt": C(xt), "pvec": pv, "expect": expect,
                 "oddmask": np.full((128, 1), -1 if half else 0, np.int32)}
            m.update(shared)
            in_maps.append(m)
    return in_maps


class Sem:
    def __init__(self, stack, nc, name):
        self.h = stack.enter_context(nc.semaphore(name))
        self.n = 0
        self.name = name

    def inc(self, k=1):
        self.n += k
        return self.n


class EM:
    ENG = ("tensor", "vector", "scalar", "gpsimd", "sync")

    def __init__(self):
        self.q = {e: [] for e in self.ENG}
        self.waited = {e: {} for e in self.ENG}
        self.ready = {}

    def emit(self, eng, fn):
        self.q[eng].append(fn)

    def wait(self, eng, sem, val):
        if val <= 0:
            return
        w = self.waited[eng]
        if w.get(sem.name, 0) >= val:
            return
        w[sem.name] = val
        fn = lambda e, s=sem.h, v=val: e.wait_ge(s, v)
        fn.is_wait = True
        self.q[eng].append(fn)

    def mil(self, eng, sem):
        last = self.q[eng].pop()
        assert not getattr(last, "is_wait", False), f"mil({eng}) after wait"

        def wrapped(e, last=last, s=sem.h):
            r = last(e)
            r.then_inc(s, 1)
            return r
        self.q[eng].append(wrapped)
        return sem.inc()

    def mark(self, key, sem, val):
        self.ready[key] = (sem, val)

    def need(self, eng, *keys):
        for k in keys:
            if k in self.ready:
                s, v = self.ready[k]
                self.wait(eng, s, v)


TENSORS = {}


def build_program(debug=(), nheads=H, nchunks=NCH, detect_races=False,
                  no_stage_d=False):
    nc = bass.Bass("TRN2", target_bir_lowering=False, num_devices=8,
                   detect_race_conditions=detect_races)
    ap = lambda t: t.ap()
    xt_d = ap(nc.dram_tensor("xt", [D, TE], F32, kind="ExternalInput"))
    pvec_d = ap(nc.dram_tensor("pvec", [1, 8], I32, kind="ExternalInput"))
    expect_d = ap(nc.dram_tensor("expect", [1, H], I32, kind="ExternalInput"))
    oddm_d = ap(nc.dram_tensor("oddmask", [128, 1], I32, kind="ExternalInput"))
    w_d = {k: ap(nc.dram_tensor(k, [H, NJT, 128, 128], F32,
                                kind="ExternalInput"))
           for k in ("wq", "wk", "wv", "wg")}
    wo_d = ap(nc.dram_tensor("wo", [16, 16, 128, 128], F32, kind="ExternalInput"))
    rpack_d = ap(nc.dram_tensor("rpack", [NJT, 128, 96], F32, kind="ExternalInput"))
    rbias_d = ap(nc.dram_tensor("rbias", [96, 1], F32, kind="ExternalInput"))
    eb_d = ap(nc.dram_tensor("eb", [64, D], F32, kind="ExternalInput"))
    convw_d = ap(nc.dram_tensor("convw", [128, H * 12], F32, kind="ExternalInput"))
    convb_d = ap(nc.dram_tensor("convb", [128, H * 3], F32, kind="ExternalInput"))
    lncol_d = ap(nc.dram_tensor("lncol", [128, 2], F32, kind="ExternalInput"))
    idf_d = ap(nc.dram_tensor("idf", [128, 128], F32, kind="ExternalInput"))
    triui_d = ap(nc.dram_tensor("triu_i", [128, 128], F32, kind="ExternalInput"))
    trius_d = ap(nc.dram_tensor("triu_s", [128, 128], F32, kind="ExternalInput"))
    trils_d = ap(nc.dram_tensor("tril_s", [128, 128], F32, kind="ExternalInput"))
    msk_d = ap(nc.dram_tensor("msk", [128, 256], F32, kind="ExternalInput"))
    ones1_d = ap(nc.dram_tensor("ones1", [1, 128], F32, kind="ExternalInput"))
    onescol_d = ap(nc.dram_tensor("onescol", [128, 1], F32, kind="ExternalInput"))
    out_d = ap(nc.dram_tensor("outT", [D, TC], F32, kind="ExternalOutput"))
    og_d = ap(nc.dram_tensor("og_scr", [H * 128, TC], BF16, kind="Internal"))
    xchgA = ap(nc.dram_tensor("xchgA", [H * 128, 128], F32, kind="Internal",
                              addr_space="Shared"))
    flagA = ap(nc.dram_tensor("flagA", [1, H], I32, kind="Internal",
                              addr_space="Shared"))
    dbg = {}
    for name, shape, dt in debug:
        dbg[name] = ap(nc.dram_tensor("dbg_" + name, shape, dt,
                                      kind="ExternalOutput"))

    em = EM()
    st = contextlib.ExitStack()
    with st:
        sb = lambda n, sh, dt: st.enter_context(nc.sbuf_tensor(n, sh, dt))
        psum = lambda n, sh, dt: st.enter_context(nc.psum_tensor(n, sh, dt))

        IDF = sb("IDF", [128, 128], F32)
        IDB = sb("IDB", [128, 128], BF16)
        TRIU_I = sb("TRIU_I", [128, 128], F32)
        TRIU_S = sb("TRIU_S", [128, 128], F32)
        TRIL_S = sb("TRIL_S", [128, 128], F32)
        MSK = sb("MSK", [128, 256], BF16)
        ONES1 = sb("ONES1", [1, 128], F32)
        ONESC = sb("ONESC", [128, 1], BF16)
        LNCOL = sb("LNCOL", [128, 2], F32)
        RBIAS = sb("RBIAS", [96, 1], F32)
        CONVW = sb("CONVW", [128, H * 12], F32)
        CONVBH = sb("CONVBH", [128, H * 3], F32)
        PVEC = sb("PVEC", [1, 8], I32)
        EXPECT = sb("EXPECT", [1, H], I32)
        ODDM = sb("ODDM", [128, 1], I32)
        EBS = sb("EBS", [64, D], BF16)
        RPACK = sb("RPACK", [128, NJT * 96], BF16)
        FLG = sb("FLG", [1, 4], I32)
        ZEROC = sb("ZEROC", [128, 128], F32)
        DBGB = sb("DBGB", [128, 4 * 272], F32) if debug else None
        CB = sb("CB", [128, 2], F32)

        XBF = sb("XBF", [128, NJT * TE], BF16)
        MTS = sb("MTS", [64, TC], BF16)
        SWB = sb("SWB", [128, TC], BF16)
        MUR = sb("MUR", [1, TE], F32)
        RSR = sb("RSR", [1, TE], F32)
        LGAB = sb("LGAB", [128, NCH * 32], F32)
        ECOL = sb("ECOL", [128, NCH * 96], F32)
        ST = sb("ST", [128, 128], F32)
        STB = sb("STB", [128, 128], BF16)
        RECV = sb("RECV", [128, 128], F32)
        VARB = sb("VARB", [128, 128], F32)
        EROWB = sb("EROWB", [16, 256], F32)
        EROW2 = sb("EROW2", [1, 512], F32)
        ROWS = sb("ROWS", [1, 544], F32)

        AR = sb("ARENA", [128, 46496], BF16)

        def car(off, n):
            return AR[:, off:off + n]

        def far(off, n):
            return AR[:].bitcast(F32)[:, off:off + n]


        o = 0
        def take(n):
            nonlocal o
            r = o
            o += n
            return r
        KPRE = car(take(TE + 1), TE)
        QPRE = car(take(TE + 1), TE)
        VPRE = car(take(TE + 1), TE)
        KT = car(take(TC), TC)
        QT = car(take(TC), TC)
        VT = car(take(TC), TC)
        GTH = car(take(TC), TC)
        OGS = car(take(TC), TC)
        DIAG = car(take(12 * 128), 12 * 128)
        WSB = {k: [car(take(NJT * 128), NJT * 128) for p in range(2)]
               for k in ("wq", "wk", "wv", "wg")}
        NSLOT = 3
        PBT = [car(take(128), 128) for _ in range(NSLOT)]
        U0S = [car(take(128), 128) for _ in range(NSLOT)]
        WTS = [car(take(128), 128) for _ in range(NSLOT)]
        GKS = [car(take(128), 128) for _ in range(NSLOT)]
        RS = [car(take(256), 256) for _ in range(NSLOT)]
        APK = [car(take(256), 256) for _ in range(NSLOT)]
        XS = [car(take(256), 256) for _ in range(2)]
        ET = car(take(256), 256)
        UB = car(take(128), 128)
        OLBD = car(take(256), 256)
        THB = car(take(512), 512)
        f_off = (o + 2) // 2
        oXSQB = o
        XSQB = car(take(1040), 1040)
        THB2 = car(oXSQB, 512)
        THBD = [THB, THB2]
        MT1 = car(take(64), 64)
        f_off = (o + 2) // 2
        OT = far(f_off, 256); f_off += 256
        ECL = far(f_off, 256); f_off += 256
        SCR = far(f_off, 256); f_off += 256
        SCR2 = far(f_off, 256); f_off += 256
        ZT = far(f_off, 96); f_off += 96
        OB = far(f_off, NCH * 128); f_off += NCH * 128
        c_end = 2 * f_off
        o = 0
        OGT = car(take(16 * 2048), 16 * 2048)
        WOD = [car(take(16 * 128), 16 * 128) for _ in range(2)]
        f2 = (o + 2) // 2
        XRT = [far(f2 + i * 512, 512) for i in range(2)]; f2 += 1024
        OUTS = [far(f2 + i * 512, 512) for i in range(2)]; f2 += 1024
        FFN = [far(f2 + i * 512, 512) for i in range(2)]; f2 += 1024
        THD = far(f2, 512); f2 += 512
        d_end = 2 * f2
        assert max(c_end, d_end) <= 46496, (c_end, d_end)

        PA = psum("PA", [128, 512], F32)
        PB = psum("PB", [128, 512], F32)
        PC = psum("PC", [128, 256], F32)
        PD = psum("PD", [128, 256], F32)
        PEb = psum("PEb", [128, 256], F32)
        PF = psum("PF", [128, 256], BF16)
        PF2 = PF[:, 128:256]
        PG = psum("PG", [128, 128], F32)
        PH = psum("PH", [1, 512], F32)

        sX = Sem(st, nc, "sX")
        sWC = Sem(st, nc, "sWC")
        sXL = Sem(st, nc, "sXL")
        sWa = Sem(st, nc, "sWa")
        sWb = Sem(st, nc, "sWb")
        sOG = Sem(st, nc, "sOG")
        sXR = Sem(st, nc, "sXR")
        sT = Sem(st, nc, "sT")
        sV = Sem(st, nc, "sV")
        sS = Sem(st, nc, "sS")
        sG = Sem(st, nc, "sG")
        sO = Sem(st, nc, "sO")
        sP = Sem(st, nc, "sP")
        sE = Sem(st, nc, "sE")
        sE2 = Sem(st, nc, "sE2")
        sEA = Sem(st, nc, "sEA")
        sR = Sem(st, nc, "sR")
        sPoll = st.enter_context(nc.semaphore("sPoll"))

        E = em.emit
        W = em.wait
        NEED = em.need
        MARK = em.mark

        def tmil():
            return em.mil("tensor", sT)

        def vmil():
            return em.mil("vector", sV)

        def vedge():
            W("vector", sV, vmil())

        for dst, src in [(IDF[:], idf_d), (TRIU_I[:], triui_d),
                         (TRIU_S[:], trius_d), (TRIL_S[:], trils_d),
                         (RBIAS[:], rbias_d), (CONVW[:], convw_d),
                         (LNCOL[:], lncol_d),
                         (PVEC[:], pvec_d), (EXPECT[:], expect_d),
                         (ODDM[:], oddm_d)]:
            E("sync", lambda e, d=dst, s=src: e.dma_start(out=d, in_=s)
              .then_inc(sX.h, 16))
            sX.inc(16)
        for dst, src in [(IDB[:], idf_d), (MSK[:], msk_d),
                         (ONESC[:], onescol_d), (EBS[:], eb_d),
                         (CONVBH[:], convb_d)]:
            E("gpsimd", lambda e, d=dst, s=src: e.dma_start(out=d, in_=s)
              .then_inc(sWC.h, 16))
            sWC.inc(16)
        E("gpsimd", lambda e: e.dma_start(
            out=RPACK[:].rearrange("p (j c) -> p j c", j=NJT),
            in_=rpack_d.rearrange("j p c -> p j c")).then_inc(sWC.h, 16))
        sWC.inc(16)
        for dt in range(NJT):
            E("gpsimd", lambda e, dt=dt: e.dma_start(
                out=XBF[:, dt * TE:(dt + 1) * TE],
                in_=xt_d[dt * 128:(dt + 1) * 128, :]).then_inc(sXL.h, 16))
            sXL.inc(16)
        x_loaded = sXL.n
        consts_loaded = sX.n

        W("vector", sX, consts_loaded)
        W("vector", sWC, sWC.n)
        E("vector", lambda e: e.memset(ONES1[:], 1.0))
        E("vector", lambda e: e.memset(ZEROC[:], 0.0))
        E("vector", lambda e: e.memset(CB[:, 0:1], 1e-5))
        E("vector", lambda e: e.memset(RECV[:], 0.0))
        E("vector", lambda e: e.memset(MTS[:], 0.0))
        E("vector", lambda e: e.memset(SWB[:], 0.0))
        E("vector", lambda e: e.memset(OGS[:], 0.0))
        E("vector", lambda e: e.memset(CB[:, 1:2], 1.0))
        E("vector", lambda e: e.tensor_scalar(CONVBH[:], CONVBH[:], 0.5, None,
                                              ALU.mult))
        E("vector", lambda e: e.tensor_scalar(LNCOL[:], LNCOL[:], 0.5, None,
                                              ALU.mult))
        consts2 = vmil()
        MARK("consts", sV, consts2)

        myoff = PVEC[0:1, 1:2]
        W("gpsimd", sX, consts_loaded)
        with nc.gpsimd.register("rPar") as rPar, \
             nc.gpsimd.register("rC") as rC, \
             nc.gpsimd.register("rF") as rF, \
             nc.gpsimd.register("rEx") as rEx, \
             nc.gpsimd.register("rTgt") as rTgt, \
             nc.gpsimd.register("rCap") as rCap:

            E("gpsimd", lambda e: (e.reg_load(rPar, PVEC[0:1, 0:1]),
                                   e.reg_mov(rTgt, 16)))
            W("gpsimd", sV, consts2)
            def g_zero(e):
                with e.If_eq(rPar, 0):
                    e.dma_start(out=flagA[0:1, 0:H],
                                in_=ZEROC[0:1, 0:H].bitcast(I32)
                                ).then_inc(sEA.h, 16)
                    e.wait_ge(sEA.h, 16)
            E("gpsimd", g_zero)
            sEA.inc(16)
            zero_done = sE.n

            groups5 = [(0, 512), (512, 512), (1024, 512), (1536, 512),
                       (2048, 3)]
            W("tensor", sXL, x_loaded)
            W("tensor", sWC, sWC.n)
            W("scalar", sXL, x_loaded)
            for g0, gn in groups5:
                for dt in range(NJT):
                    E("tensor", lambda e, dt=dt, g0=g0, gn=gn: e.matmul(
                        PH[0:1, 0:gn], ONESC[:],
                        XBF[:, dt * TE + g0:dt * TE + g0 + gn],
                        start=(dt == 0), stop=(dt == NJT - 1)))
                m1 = tmil()
                W("vector", sT, m1)
                E("vector", lambda e, g0=g0, gn=gn: e.tensor_scalar(
                    MUR[0:1, g0:g0 + gn], PH[0:1, 0:gn], 1.0 / D, None,
                    ALU.mult))
                mu_ev = vmil()
                W("tensor", sV, mu_ev)
                for dt in range(NJT):
                    sl = (dt % 2) * 520
                    if dt >= 2:
                        W("scalar", sT, sT.n - 1)
                    else:
                        W("scalar", sT, sT.n)
                    E("scalar", lambda e, dt=dt, g0=g0, gn=gn, sl=sl:
                      e.activation(XSQB[:, sl:sl + gn],
                                   XBF[:, dt * TE + g0:dt * TE + g0 + gn],
                                   AF.Square).then_inc(sS.h, 1))
                    sqd = sS.inc()
                    W("tensor", sS, sqd)
                    E("tensor", lambda e, gn=gn, dt=dt, sl=sl: e.matmul(
                        PH[0:1, 0:gn], ONESC[:], XSQB[:, sl:sl + gn],
                        start=(dt == 0), stop=(dt == NJT - 1))
                      .then_inc(sT.h, 1))
                    sT.inc()
                m2 = sT.n
                W("vector", sT, m2)
                E("vector", lambda e, g0=g0, gn=gn: e.tensor_tensor(
                    ROWS[0:1, 0:gn], MUR[0:1, g0:g0 + gn],
                    MUR[0:1, g0:g0 + gn], ALU.mult))
                E("vector", lambda e, g0=g0, gn=gn: e.scalar_tensor_tensor(
                    RSR[0:1, g0:g0 + gn], PH[0:1, 0:gn], 1.0 / D,
                    ROWS[0:1, 0:gn], ALU.mult, ALU.subtract))
                var_ev = vmil()
                W("tensor", sV, var_ev)
            W("scalar", sV, sV.n)
            W("scalar", sV, consts2)
            E("scalar", lambda e: e.activation(RSR[0:1, 0:TE], RSR[0:1, 0:TE],
                                               AF.Ln, bias=CB[0:1, 0:1]))
            E("scalar", lambda e: e.activation(RSR[0:1, 0:TE], RSR[0:1, 0:TE],
                                               AF.Exp, scale=-0.5)
              .then_inc(sS.h, 1))
            rstd_done = sS.inc()
            W("tensor", sS, rstd_done)
            for g0, gn in groups5:
                E("tensor", lambda e, g0=g0, gn=gn: e.matmul(
                    PA[:, 0:gn], ONES1[:], MUR[0:1, g0:g0 + gn],
                    start=True, stop=True))
                E("tensor", lambda e, g0=g0, gn=gn: e.matmul(
                    PB[:, 0:gn], ONES1[:], RSR[0:1, g0:g0 + gn],
                    start=True, stop=True))
                bc = tmil()
                W("vector", sT, bc)
                for dt in range(NJT):
                    E("vector", lambda e, dt=dt, g0=g0, gn=gn: e.tensor_tensor(
                        XBF[:, dt * TE + g0:dt * TE + g0 + gn],
                        XBF[:, dt * TE + g0:dt * TE + g0 + gn],
                        PA[:, 0:gn], ALU.subtract))
                    E("vector", lambda e, dt=dt, g0=g0, gn=gn: e.tensor_tensor(
                        XBF[:, dt * TE + g0:dt * TE + g0 + gn],
                        XBF[:, dt * TE + g0:dt * TE + g0 + gn],
                        PB[:, 0:gn], ALU.mult))
                xn_g = vmil()
                W("tensor", sV, xn_g)
            xn_done = sV.n
            MARK("xn", sV, xn_done)

            for kname in ("wq", "wk", "wv", "wg"):
                E("gpsimd", lambda e, k=kname: e.dma_start(
                    out=WSB[k][0].rearrange("p (j c) -> p j c", j=NJT),
                    in_=w_d[k][0].rearrange("j p c -> p j c"))
                  .then_inc(sWa.h, 16))
                sWa.inc(16)
            wload = {0: (sWa, sWa.n)}

            W("tensor", sV, xn_done)
            W("tensor", sWC, sWC.n)
            W("tensor", sX, consts_loaded)
            W("scalar", sX, consts_loaded)
            def gen_SB():
                for c in range(nchunks):
                    t0 = 3 + c * CH
                    NEED("tensor", "PCb")
                    for jt in range(NJT):
                        E("tensor", lambda e, jt=jt, t0=t0: e.matmul(
                            PC[0:96, 0:CH], RPACK[:, jt * 96:(jt + 1) * 96],
                            XBF[:, jt * TE + t0:jt * TE + t0 + CH],
                            start=(jt == 0), stop=(jt == NJT - 1)))
                    zmm = tmil()
                    W("scalar", sT, zmm)
                    E("scalar", lambda e: e.activation(
                        SCR[0:96, 0:CH], PC[0:96, 0:CH], AF.Identity,
                        bias=RBIAS[:, 0:1]).then_inc(sS.h, 1))
                    zb = sS.inc()
                    MARK("PCb", sS, zb)
                    W("tensor", sS, zb)
                    E("tensor", lambda e: e.transpose(PD[0:128, 0:96],
                                                      SCR[0:96, 0:CH],
                                                      IDF[0:96, 0:96]))
                    ztr = tmil()
                    W("vector", sT, ztr)
                    E("vector", lambda e: e.tensor_copy(ZT[:, 0:96],
                                                        PD[:, 0:96]))
                    zev = vmil()
                    W("tensor", sV, zev)
                    yield

                    E("vector", lambda e: e.tensor_scalar(
                        SCR2[:, 0:32], ZT[:, 0:32], -1.0, None, ALU.mult))
                    vedge()
                    E("vector", lambda e: e.tensor_tensor(
                        SCR2[:, 0:32], SCR2[:, 0:32], ZT[:, 0:32], ALU.max))
                    absd = vmil()
                    W("scalar", sV, absd)
                    E("scalar", lambda e: e.activation(SCR2[:, 32:64],
                                                       SCR2[:, 0:32], AF.Exp,
                                                       scale=-1.0))
                    E("scalar", lambda e: e.activation(SCR2[:, 32:64],
                                                       SCR2[:, 32:64], AF.Ln,
                                                       bias=CB[:, 1:2]).then_inc(sS.h, 1))
                    lnd = sS.inc()
                    W("vector", sS, lnd)
                    E("vector", lambda e: e.tensor_scalar(
                        SCR2[:, 64:96], ZT[:, 0:32], 0.0, None, ALU.min))
                    vedge()
                    E("vector", lambda e, c=c: e.tensor_tensor(
                        LGAB[:, c * 32:(c + 1) * 32], SCR2[:, 64:96],
                        SCR2[:, 32:64], ALU.subtract))
                    lgd = vmil()
                    if c == nchunks - 1:
                        MARK("lgabF", sV, lgd)

                    W("tensor", sV, lgd)
                    NEED("tensor", "PEb")
                    E("tensor", lambda e, c=c: e.matmul(
                        PEb[:, 0:16], TRIU_S[:], LGAB[:, c * 32:c * 32 + 16],
                        start=True, stop=True))
                    E("tensor", lambda e, c=c: e.matmul(
                        PEb[:, 16:32], TRIU_I[:], LGAB[:, c * 32:c * 32 + 16],
                        start=True, stop=True))
                    E("tensor", lambda e, c=c: e.matmul(
                        PEb[:, 32:48], TRIL_S[:], LGAB[:, c * 32:c * 32 + 16],
                        start=True, stop=True))
                    csd = tmil()
                    W("vector", sT, csd)
                    E("vector", lambda e, c=c: e.tensor_tensor(
                        SCR[:, 0:16], PEb[:, 0:16],
                        LGAB[:, c * 32 + 16:c * 32 + 32], ALU.add))
                    E("vector", lambda e: e.tensor_copy(SCR[:, 16:32],
                                                        PEb[:, 16:32]))
                    E("vector", lambda e: e.tensor_copy(SCR[:, 32:48],
                                                        PEb[:, 32:48]))
                    E("vector", lambda e, c=c: e.tensor_scalar(
                        ECOL[:, c * 96 + 80:c * 96 + 96], PEb[:, 16:32], -1.0,
                        None, ALU.mult))
                    lce = vmil()
                    W("scalar", sV, lce)
                    W("tensor", sV, lce)
                    E("scalar", lambda e, c=c: e.activation(
                        ECOL[:, c * 96 + 0:c * 96 + 16], SCR[:, 0:16], AF.Exp))
                    E("scalar", lambda e, c=c: e.activation(
                        ECOL[:, c * 96 + 16:c * 96 + 32], SCR[:, 32:48], AF.Exp))
                    E("scalar", lambda e, c=c: e.activation(
                        ECOL[:, c * 96 + 32:c * 96 + 48], SCR[:, 16:32], AF.Exp))
                    E("scalar", lambda e, c=c: e.activation(
                        ECOL[:, c * 96 + 64:c * 96 + 80],
                        LGAB[:, c * 32 + 16:c * 32 + 32], AF.Exp)
                      .then_inc(sS.h, 1))
                    ecd = sS.inc()
                    W("vector", sS, ecd)
                    E("vector", lambda e, c=c: e.tensor_tensor(
                        ECOL[:, c * 96 + 48:c * 96 + 64],
                        ECOL[:, c * 96 + 16:c * 96 + 32],
                        ECOL[:, c * 96 + 32:c * 96 + 48], ALU.mult))
                    E("vector", lambda e, c=c: e.tensor_scalar(
                        ECOL[:, c * 96 + 0:c * 96 + 16],
                        ECOL[:, c * 96 + 0:c * 96 + 16], DKS, None, ALU.mult))
                    E("vector", lambda e, c=c: e.tensor_scalar(
                        ECOL[:, c * 96 + 16:c * 96 + 32],
                        ECOL[:, c * 96 + 16:c * 96 + 32], DKS, None, ALU.mult))
                    ecv = vmil()
                    yield

                    W("vector", sV, zev)
                    E("vector", lambda e: e.reduce_max(SCR2[:, 96:97],
                                                       ZT[:, 32:96], AX))
                    rmx = em.mil("vector", sV)
                    W("vector", sV, rmx)
                    E("vector", lambda e: e.tensor_scalar(
                        SCR2[:, 97:98], SCR2[:, 96:97], -1.0, None, ALU.mult)
                      .then_inc(sV.h, 1))
                    nmx = sV.inc()
                    W("scalar", sV, nmx)
                    E("scalar", lambda e: e.activation(
                        SCR[:, 128:192], ZT[:, 32:96], AF.Exp,
                        bias=SCR2[:, 97:98]).then_inc(sS.h, 1))
                    etd = sS.inc()
                    W("vector", sS, etd)
                    E("vector", lambda e: e.reduce_sum(SCR2[:, 98:99],
                                                       SCR[:, 128:192], AX))
                    rzs = em.mil("vector", sV)
                    E("vector", lambda e: e.tensor_copy(SCR[:, 192:256],
                                                        SCR[:, 128:192]))
                    vedge()
                    for it in range(7):
                        E("vector", lambda e: e.reduce_max(SCR2[:, 99:100],
                                                           SCR[:, 192:256], AX))
                        rtm = em.mil("vector", sV)
                        W("vector", sV, rtm)
                        E("vector", lambda e: e.scalar_tensor_tensor(
                            SCR[:, 192:256], SCR[:, 192:256], SCR2[:, 99:100],
                            SCR[:, 192:256], ALU.is_lt, ALU.mult))
                        vedge()
                    E("vector", lambda e: e.reduce_max(SCR2[:, 99:100],
                                                       SCR[:, 192:256], AX))
                    rt8 = em.mil("vector", sV)
                    W("vector", sV, rt8)
                    W("vector", sV, rzs)
                    E("vector", lambda e: e.reciprocal(SCR2[:, 102:103],
                                                       SCR2[:, 98:99]))
                    vedge()
                    E("vector", lambda e: e.tensor_scalar(
                        SCR2[:, 103:167], SCR[:, 128:192], SCR2[:, 99:100], None,
                        ALU.is_ge))
                    vedge()
                    E("vector", lambda e: e.tensor_tensor(
                        SCR2[:, 103:167], SCR2[:, 103:167], SCR[:, 128:192],
                        ALU.mult))
                    vedge()
                    E("vector", lambda e: e.tensor_scalar(
                        MT1[:, 0:64], SCR2[:, 103:167], SCR2[:, 102:103], None,
                        ALU.mult))
                    mbf = vmil()
                    W("tensor", sV, mbf)
                    NEED("tensor", "PFb")
                    E("tensor", lambda e: e.transpose(PF[0:64, 0:128],
                                                      MT1[:, 0:64], IDB[:]))
                    mtr = tmil()
                    W("vector", sT, mtr)
                    E("vector", lambda e, c=c: e.tensor_copy(
                        MTS[0:64, c * CH:(c + 1) * CH], PF[0:64, 0:128]))
                    mev = vmil()
                    MARK("PFb", sV, mev)
                    W("tensor", sV, mev)
                    NEED("tensor", "PH")
                    E("tensor", lambda e, c=c: e.matmul(
                        PH[0:1, 0:CH], ONESC[0:64, :],
                        MTS[0:64, c * CH:(c + 1) * CH], start=True, stop=True))
                    swm = tmil()
                    W("vector", sT, swm)
                    E("vector", lambda e: e.tensor_copy(
                        ROWS[0:1, 0:CH], PH[0:1, 0:CH]))
                    swr = vmil()
                    MARK("PH", sV, swr)
                    W("tensor", sV, swr)
                    NEED("tensor", "PEb")
                    E("tensor", lambda e: e.matmul(
                        PEb[:, 0:CH], ONES1[:], ROWS[0:1, 0:CH],
                        start=True, stop=True))
                    bsw = tmil()
                    W("vector", sT, bsw)
                    E("vector", lambda e, c=c: e.tensor_copy(
                        SWB[:, c * CH:(c + 1) * CH], PEb[:, 0:CH]))
                    sb_ev = vmil()
                    MARK("PEb", sV, sb_ev)
                    MARK("PC", sS, zb)
                    MARK("PD", sV, zev)
                    yield


            c4done_abs = 0
            spill_done = 0

            def rr(*gen_list):
                gens = [g for g in gen_list if g is not None]
                while gens:
                    alive = []
                    for g in gens:
                        try:
                            next(g)
                            alive.append(g)
                        except StopIteration:
                            pass
                    gens = alive

            rr(gen_SB())
            pending3 = None
            for h in range(nheads):
                par = h % 2
                red_mil = []
                ebc_mil = {}
                exn = {}
                erowb_ref = [0]

                def gen_P1(h=h, par=par, prev3=pending3, first=(h == 0)):
                    if h + 1 < nheads:
                        sPar = sWb if (h + 1) % 2 else sWa
                        for kname in ("wq", "wk", "wv", "wg"):
                            if h >= 1:
                                NEED("gpsimd", f"projdone{h - 1}")
                            E("gpsimd", lambda e, k=kname, hh=h + 1,
                              pp=(h + 1) % 2, s=sPar: e.dma_start(
                                out=WSB[k][pp].rearrange(
                                    "p (j c) -> p j c", j=NJT),
                                in_=w_d[k][hh].rearrange("j p c -> p j c"))
                              .then_inc(s.h, 16))
                            sPar.inc(16)
                        wload[h + 1] = (sPar, sPar.n)
                    yield

                    def erow_batch(h=h):
                        lga = LGAB[:].rearrange("p (c k) -> p c k", k=32)
                        lhs_a = lga[:, 0:nchunks, h:h + 1].rearrange(
                            "p c one -> p (c one)")
                        lhs_b = lga[:, 0:nchunks, 16 + h:17 + h].rearrange(
                            "p c one -> p (c one)")
                        NEED("tensor", "lgabF")
                        NEED("tensor", "PEb")
                        E("tensor", lambda e, lhs_a=lhs_a, nb=nchunks:
                          e.matmul(PEb[0:nb, 0:128], lhs_a, TRIU_S[:],
                                   start=True, stop=False))
                        E("tensor", lambda e, lhs_b=lhs_b, nb=nchunks:
                          e.matmul(PEb[0:nb, 0:128], lhs_b, IDF[:],
                                   start=False, stop=True))
                        E("tensor", lambda e, lhs_a=lhs_a, nb=nchunks:
                          e.matmul(PEb[0:nb, 128:256], lhs_a, TRIU_I[:],
                                   start=True, stop=True))
                        ermb = tmil()
                        W("scalar", sT, ermb)
                        W("scalar", sE, sE.n)
                        W("scalar", sE2, sE2.n)
                        E("scalar", lambda e, nb=nchunks: e.activation(
                            EROWB[0:nb, 0:256], PEb[0:nb, 0:256],
                            AF.Identity).then_inc(sS.h, 1))
                        erowb_ref[0] = sS.inc()
                        MARK("PEb", sS, erowb_ref[0])

                    if not first:
                        erow_batch()
                        yield

                    W("tensor", wload[h][0], wload[h][1])
                    prj = [("wq", QPRE, False), ("wk", KPRE, False),
                           ("wv", VPRE, False), ("wg", None, True)]
                    groups5 = [(0, 512), (512, 512), (1024, 512),
                               (1536, 512), (2048, 3)]
                    ggroups = [(3, 512), (515, 512), (1027, 512), (1539, 512)]
                    pp = 0
                    for kname, dstpre, is_g in prj:
                        if is_g and prev3 is not None:
                            for _ in prev3:
                                pass
                        if is_g and first:
                            erow_batch()
                        gs = ggroups if is_g else groups5
                        for gi, (g0, gn) in enumerate(gs):
                            PSU = PA if pp % 2 == 0 else PB
                            pp += 1
                            NEED("tensor", "PA" if PSU is PA else "PB")
                            for jt in range(NJT):
                                E("tensor", lambda e, jt=jt, g0=g0, gn=gn,
                                  P=PSU, k=kname, par=par: e.matmul(
                                    P[:, 0:gn],
                                    WSB[k][par][:, jt * 128:(jt + 1) * 128],
                                    XBF[:, jt * TE + g0:jt * TE + g0 + gn],
                                    start=(jt == 0), stop=(jt == NJT - 1)))
                            pm = tmil()
                            if is_g:
                                W("scalar", sT, pm)
                                NEED("scalar", "GTHfree")
                                E("scalar", lambda e, g0=g0, gn=gn, P=PSU:
                                  e.activation(GTH[:, g0 - 3:g0 - 3 + gn],
                                               P[:, 0:gn], AF.Tanh, scale=0.5)
                                  .then_inc(sS.h, 1))
                                ev = sS.inc()
                                MARK("PA" if PSU is PA else "PB", sS, ev)
                            else:
                                W("vector", sT, pm)
                                E("vector", lambda e, g0=g0, gn=gn, P=PSU,
                                  d=dstpre: e.tensor_copy(d[:, g0:g0 + gn],
                                                          P[:, 0:gn]))
                                ev = vmil()
                                MARK("PA" if PSU is PA else "PB", sV, ev)
                            yield
                    MARK(f"projdone{h}", sV, sV.n)
                    NEED("vector", "consts")
                    for tn in range(3):
                        for tau in range(4):
                            E("vector", lambda e, tn=tn, tau=tau, h=h:
                              e.tensor_scalar(
                                DIAG[:, (tn * 4 + tau) * 128:
                                     (tn * 4 + tau + 1) * 128],
                                IDB[:], CONVW[:, h * 12 + tn * 4 + tau:
                                              h * 12 + tn * 4 + tau + 1],
                                0.5, ALU.mult, ALU.mult))
                    diagd = vmil()
                    W("tensor", sV, diagd)
                    yield

                    pres = [QPRE, KPRE, VPRE]
                    outs = [QT, KT, VT]
                    pp = 0
                    for gi, (g0o, gn) in enumerate(ggroups):
                        for tn in range(3):
                            PSU = PA if pp % 2 == 0 else PB
                            TH = THBD[pp % 2]
                            pp += 1
                            NEED("tensor", "PA" if PSU is PA else "PB")
                            for tau in range(4):
                                E("tensor", lambda e, tn=tn, tau=tau, g0o=g0o,
                                  gn=gn, P=PSU: e.matmul(
                                    P[:, 0:gn],
                                    DIAG[:, (tn * 4 + tau) * 128:
                                         (tn * 4 + tau + 1) * 128],
                                    pres[tn][:, g0o - 3 + tau:
                                             g0o - 3 + tau + gn],
                                    start=(tau == 0), stop=(tau == 3)))
                            cm = tmil()
                            W("scalar", sT, cm)
                            NEED("scalar", "THBw%d" % ((pp - 1) % 2))
                            E("scalar", lambda e, tn=tn, gn=gn, P=PSU, h=h,
                              TH=TH: e.activation(TH[0:128, 0:gn], P[:, 0:gn],
                                                  AF.Tanh,
                                                  bias=CONVBH[:, h * 3 + tn:
                                                              h * 3 + tn + 1])
                              .then_inc(sS.h, 1))
                            thd = sS.inc()
                            W("vector", sS, thd)
                            E("vector", lambda e, gn=gn, TH=TH:
                              e.tensor_scalar(
                                TH[:, 0:gn], TH[:, 0:gn], 1.0, None,
                                ALU.add))
                            E("vector", lambda e, tn=tn, g0o=g0o, gn=gn,
                              P=PSU, h=h, TH=TH: e.scalar_tensor_tensor(
                                outs[tn][:, g0o - 3:g0o - 3 + gn], P[:, 0:gn],
                                CONVBH[:, h * 3 + tn:h * 3 + tn + 1],
                                TH[:, 0:gn], ALU.add, ALU.mult))
                            ev = vmil()
                            MARK("PA" if PSU is PA else "PB", sV, ev)
                            MARK("THBw%d" % ((pp - 1) % 2), sV, ev)
                        MARK(f"cv{h}_{gi}", sV, sV.n)
                        if gi == 0:
                            yield "chunks_ok"
                        else:
                            yield

                g1 = gen_P1()
                p3 = pending3
                pending3 = None
                while g1 is not None:
                    stop_here = False
                    for g in (p3, g1):
                        if g is None:
                            continue
                        try:
                            r = next(g)
                            if g is g1 and r == "chunks_ok":
                                stop_here = True
                        except StopIteration:
                            if g is p3:
                                p3 = None
                            else:
                                g1 = None
                    if stop_here:
                        break
                carry = [g for g in (p3, g1) if g is not None]

                rtgt_recv = sR.inc(16)
                def g_recv(e, h=h, rt=rtgt_recv):
                    with e.If_eq(rPar, 1):
                        e.reg_load(rEx, EXPECT[0:1, h:h + 1])
                        e.reg_mov(rC, 1)
                        e.reg_mov(rCap, 500000)
                        with e.While(rC):
                            e.dma_start(out=FLG[0:1, 0:1],
                                        in_=flagA[0:1, h:h + 1]
                                        ).then_inc(sPoll, 16)
                            e.wait_ge(sPoll, rTgt)
                            e.reg_add(rTgt, rTgt, 16)
                            e.reg_load(rF, FLG[0:1, 0:1])
                            e.reg_sub(rC, rF, rEx)
                            e.reg_sub(rCap, rCap, 1)
                            with e.If_eq(rCap, 0):
                                e.reg_mov(rC, 0)
                        e.dma_start(out=RECV[:],
                                    in_=xchgA[h * 128:(h + 1) * 128, :]
                                    ).then_inc(sR.h, 16)
                        e.wait_ge(sR.h, rt)
                E("gpsimd", g_recv)
                E("gpsimd", lambda e: e.engine_nop().then_inc(sG.h, 1))
                recv_mil = sG.inc()
                W("vector", sG, recv_mil)
                E("vector", lambda e: e.tensor_scalar(
                    ST[:].bitcast(I32), RECV[:].bitcast(I32),
                    ODDM[:, 0:1], None, ALU.bitwise_and))
                E("vector", lambda e: e.tensor_copy(STB[:], ST[:]))
                st_init = vmil()
                MARK("STB", sV, st_init)
                MARK("ST", sV, st_init)

                def gen_A(c):
                    sl = c % NSLOT
                    ec = lambda k, c=c, hh=h: ECOL[:, c * 96 + k * 16 + hh:
                                                   c * 96 + k * 16 + hh + 1]
                    NEED("tensor", f"c4t_{h}_{c - NSLOT}",
                         f"c4v_{h}_{c - NSLOT}")
                    NEED("tensor", f"cv{h}_{c // 4}")
                    NEED("tensor", "PC")
                    E("tensor", lambda e, c=c: e.matmul(
                        PC[:, 0:128], KT[:, c * CH:(c + 1) * CH],
                        KT[:, c * CH:(c + 1) * CH], start=True, stop=True))
                    E("tensor", lambda e, c=c: e.matmul(
                        PC[:, 128:256], KT[:, c * CH:(c + 1) * CH],
                        QT[:, c * CH:(c + 1) * CH], start=True, stop=True))
                    gkq = tmil()
                    W("vector", sT, gkq)
                    E("vector", lambda e, sl=sl: e.tensor_tensor(
                        APK[sl][:, 0:256], PC[:, 0:256], MSK[:], ALU.mult))
                    gev = vmil()
                    MARK("PC", sV, gev)
                    sEx = sE if c % 2 == 0 else sE2
                    if c >= 2:
                        W("sync", sT, ebc_mil[c - 2])
                    W("sync", sS, erowb_ref[0])
                    E("sync", lambda e, c=c, sEx=sEx: e.dma_start(
                        out=EROW2[0:1, (c % 2) * 256:(c % 2) * 256 + 256],
                        in_=EROWB[c:c + 1, 0:256]).then_inc(sEx.h, 16))
                    exn[c] = (sEx, sEx.inc(16))
                    yield
                    W("tensor", *exn[c])
                    NEED("tensor", "PEb")
                    E("tensor", lambda e, c=c: e.matmul(
                        PEb[:, 0:256], ONES1[:],
                        EROW2[0:1, (c % 2) * 256:(c % 2) * 256 + 256],
                        start=True, stop=True))
                    ebc = tmil()
                    ebc_mil[c] = ebc
                    yield
                    W("vector", sT, ebc)
                    NEED("vector", "ECLs")
                    E("vector", lambda e, c=c, ec=ec: e.tensor_scalar(
                        ECL[:, 0:256], PEb[:, 0:256], ec(5), 0.0,
                        ALU.add, ALU.min))
                    ecl = vmil()
                    MARK("PEb", sV, ecl)
                    W("scalar", sV, ecl)
                    NEED("scalar", "ETf")
                    E("scalar", lambda e: e.activation(
                        ET[:, 0:256], ECL[:, 0:256], AF.Exp).then_inc(sS.h, 1))
                    etile = sS.inc()
                    MARK("ECLs", sS, etile)
                    yield
                    W("gpsimd", sV, gev)
                    W("gpsimd", sS, etile)
                    E("gpsimd", lambda e, sl=sl: e.tensor_tensor(
                        APK[sl][:, 0:256], APK[sl][:, 0:256], ET[:, 0:256],
                        ALU.mult).then_inc(sG.h, 1))
                    apk_done = sG.inc()
                    MARK(f"apk{sl}", sG, apk_done)
                    MARK("ETf", sG, apk_done)
                    MARK("ECLf", sG, apk_done)
                    yield
                    NEED("tensor", "PF")
                    E("tensor", lambda e, c=c: e.transpose(
                        PF[:, 0:128], KT[:, c * CH:(c + 1) * CH], IDB[:]))
                    t1 = tmil()
                    W("scalar", sT, t1)
                    E("scalar", lambda e, sl=sl, c=c, ec=ec: e.activation(
                        RS[sl][:, 0:128], PF[:, 0:128], AF.Identity,
                        scale=ec(0)))
                    E("scalar", lambda e, sl=sl, c=c, ec=ec: e.activation(
                        GKS[sl][:], PF[:, 0:128], AF.Identity,
                        scale=ec(1)).then_inc(sS.h, 1))
                    t1e = sS.inc()
                    MARK("PF", sS, t1e)
                    yield
                    NEED("tensor", "PF")
                    E("tensor", lambda e, c=c: e.transpose(
                        PF[:, 0:128], VT[:, c * CH:(c + 1) * CH], IDB[:]))
                    t2 = tmil()
                    W("scalar", sT, t2)
                    E("scalar", lambda e, sl=sl, c=c, ec=ec: e.activation(
                        RS[sl][:, 128:256], PF[:, 0:128], AF.Identity,
                        scale=ec(4)).then_inc(sS.h, 1))
                    t2e = sS.inc()
                    MARK("PF", sS, t2e)
                    MARK(f"rs{sl}", sS, t2e)

                def gen_B(c):
                    sl = c % NSLOT
                    NEED("tensor", f"apk{sl}", f"rs{sl}")
                    for j in range(J_NEUMANN):
                        src = RS[sl] if j == 0 else XS[(j - 1) % 2]
                        PSOL = PC if j % 2 == 0 else PD
                        NEED("tensor", "PC" if j % 2 == 0 else "PD")
                        E("tensor", lambda e, sl=sl, s=src, P=PSOL: e.matmul(
                            P[:, 0:256], APK[sl][:, 0:128], s[:, 0:256],
                            start=True, stop=True))
                        sm = tmil()
                        W("vector", sT, sm)
                        E("vector", lambda e, sl=sl, j=j, P=PSOL:
                          e.tensor_tensor(XS[j % 2][:, 0:256], P[:, 0:256],
                                          RS[sl][:, 0:256], ALU.add))
                        xev = vmil()
                        MARK("PC" if j % 2 == 0 else "PD", sV, xev)
                        W("tensor", sV, xev)
                        yield
                    xf = XS[(J_NEUMANN - 1) % 2]
                    NEED("tensor", "PF")
                    E("tensor", lambda e, x=xf: e.transpose(
                        PF[:, 0:128], x[:, 0:128], IDB[:]))
                    t3 = tmil()
                    W("vector", sT, t3)
                    E("vector", lambda e, sl=sl: e.tensor_copy(WTS[sl][:],
                                                               PF[:, 0:128]))
                    E("vector", lambda e, sl=sl, x=xf: e.tensor_copy(
                        U0S[sl][:], x[:, 128:256]))
                    t3e = vmil()
                    MARK("PF", sV, t3e)
                    MARK(f"set{sl}", sV, t3e)
                    E("vector", lambda e, sl=sl: e.tensor_copy(
                        PBT[sl][:], APK[sl][:, 128:256]))
                    vmil()

                def gen_C(c):
                    nonlocal c4done_abs
                    sl = c % NSLOT
                    ec = lambda k, c=c, hh=h: ECOL[:, c * 96 + k * 16 + hh:
                                                   c * 96 + k * 16 + hh + 1]
                    NEED("tensor", f"set{sl}", "STB")
                    NEED("tensor", "PG")
                    E("tensor", lambda e, sl=sl: e.matmul(
                        PG[:], WTS[sl][:], STB[:], start=True, stop=True))
                    um = tmil()
                    W("vector", sT, um)
                    E("vector", lambda e, sl=sl: e.tensor_tensor(
                        UB[:], PG[:], U0S[sl][:], ALU.subtract))
                    uev = vmil()
                    MARK("PG", sV, uev)
                    MARK("UB", sV, uev)
                    yield
                    NEED("tensor", "PA")
                    E("tensor", lambda e, c=c: e.matmul(
                        PA[:, 0:128], QT[:, c * CH:(c + 1) * CH], STB[:],
                        start=True, stop=True))
                    o1 = tmil()
                    NEED("tensor", "UB", "PB")
                    E("tensor", lambda e, sl=sl: e.matmul(
                        PB[:, 0:128], PBT[sl][:], UB[:], start=True,
                        stop=True))
                    o2 = tmil()
                    yield
                    NEED("tensor", "PG")
                    E("tensor", lambda e, sl=sl: e.matmul(
                        PG[:], GKS[sl][:], UB[:], start=True, stop=True))
                    fm = tmil()
                    W("scalar", sT, o1)
                    E("scalar", lambda e, c=c, ec=ec: e.activation(
                        OB[:, c * 128:(c + 1) * 128], PA[:, 0:128],
                        AF.Identity, scale=ec(2)).then_inc(sS.h, 1))
                    o1e = sS.inc()
                    MARK("PA", sS, o1e)
                    W("vector", sS, o1e)
                    W("vector", sT, o2)
                    E("vector", lambda e, c=c: e.tensor_tensor(
                        OB[:, c * 128:(c + 1) * 128],
                        OB[:, c * 128:(c + 1) * 128], PB[:, 0:128], ALU.add))
                    oev = vmil()
                    MARK("PB", sV, oev)
                    yield
                    W("vector", sT, fm)
                    E("vector", lambda e, c=c, ec=ec: e.scalar_tensor_tensor(
                        ST[:], ST[:], ec(3), PG[:], ALU.mult,
                        ALU.subtract))
                    E("vector", lambda e: e.tensor_copy(STB[:], ST[:]))
                    stev = vmil()
                    MARK("PG", sV, stev)
                    MARK("STB", sV, stev)
                    MARK(f"c4t_{h}_{c}", sT, fm)
                    MARK(f"c4v_{h}_{c}", sV, stev)
                    if c == nchunks - 1:
                        MARK(f"fold_{h}", sV, stev)
                    yield
                    W("scalar", sV, oev)
                    if red_mil:
                        W("scalar", sV, red_mil[-1])
                    E("scalar", lambda e, c=c: e.activation(
                        OT[:, 0:128], OB[:, c * 128:(c + 1) * 128], AF.Square)
                      .then_inc(sS.h, 1))
                    sqd = sS.inc()
                    E("vector", lambda e, c=c: e.tensor_reduce(
                        VARB[:, c:c + 1], OB[:, c * 128:(c + 1) * 128],
                        AX, ALU.add))
                    W("vector", sS, sqd)
                    E("vector", lambda e, c=c: e.tensor_reduce(
                        VARB[:, 16 + c:17 + c], OT[:, 0:128], AX, ALU.add))
                    red_mil.append(vmil())
                    if debug and c < 4:
                        E("vector", lambda e, c=c: e.tensor_copy(
                            DBGB[:, c * 272:c * 272 + 128],
                            OB[:, c * 128:(c + 1) * 128]))
                    c4done_abs += 1

                for i in range(nchunks + 2):
                    gens = []
                    if i >= 2:
                        gens.append(gen_C(i - 2))
                    if 1 <= i <= nchunks:
                        gens.append(gen_B(i - 1))
                    if i < nchunks:
                        gens.append(gen_A(i))
                    gens.extend(carry)
                    carry = []
                    while gens:
                        alive = []
                        for g in gens:
                            try:
                                next(g)
                                alive.append(g)
                            except StopIteration:
                                pass
                        gens = alive

                def gen_P3(h=h, red_mil=red_mil):
                    NEED("gpsimd", f"fold_{h}")
                    t1v = sEA.inc(16)
                    t2v = sEA.inc(16)
                    def g_pub(e, h=h, t1v=t1v, t2v=t2v):
                        with e.If_eq(rPar, 0):
                            e.dma_start(out=xchgA[h * 128:(h + 1) * 128, :],
                                        in_=ST[:]).then_inc(sEA.h, 16)
                            e.wait_ge(sEA.h, t1v)
                            e.dma_start(out=flagA[0:1, h:h + 1],
                                        in_=EXPECT[0:1, h:h + 1]
                                        ).then_inc(sEA.h, 16)
                            e.wait_ge(sEA.h, t2v)
                    E("gpsimd", g_pub)
                    E("gpsimd", lambda e: e.engine_nop().then_inc(sG.h, 1))
                    pubm = sG.inc()
                    MARK("pub", sG, pubm)
                    W("vector", sG, pubm)
                    yield

                    W("vector", sV, red_mil[-1])
                    vb = lambda a, nb=nchunks: VARB[:, a:a + nb]
                    vbi = lambda a, nb=nchunks: VARB[:, a:a + nb].bitcast(I32)
                    E("vector", lambda e, vb=vb: e.tensor_scalar(
                        vb(32), vb(0), 1.0 / 128, None, ALU.mult))
                    vedge()
                    E("vector", lambda e, vb=vb: e.tensor_tensor(
                        vb(48), vb(32), vb(32), ALU.mult))
                    vedge()
                    E("vector", lambda e, vb=vb: e.scalar_tensor_tensor(
                        vb(64), vb(16), 1.0 / 128, vb(48),
                        ALU.mult, ALU.subtract))
                    vedge()
                    E("vector", lambda e, vb=vb: e.tensor_scalar(
                        vb(64), vb(64), 1e-5, None, ALU.add))
                    vedge()
                    E("vector", lambda e, vb=vb, vbi=vbi: e.tensor_copy(
                        vb(80), vbi(64)))
                    vedge()
                    E("vector", lambda e, vb=vb: e.tensor_scalar(
                        vb(80), vb(80), -0.5, QUAKE_C, ALU.mult, ALU.add))
                    vedge()
                    E("vector", lambda e, vb=vb, vbi=vbi: e.tensor_copy(
                        vbi(96), vb(80)))
                    vedge()
                    for _ in range(2):
                        E("vector", lambda e, vb=vb: e.tensor_tensor(
                            vb(80), vb(96), vb(96), ALU.mult))
                        vedge()
                        E("vector", lambda e, vb=vb: e.tensor_tensor(
                            vb(80), vb(80), vb(64), ALU.mult))
                        vedge()
                        E("vector", lambda e, vb=vb: e.tensor_scalar(
                            vb(80), vb(80), -0.5, 1.5, ALU.mult, ALU.add))
                        vedge()
                        E("vector", lambda e, vb=vb: e.tensor_tensor(
                            vb(96), vb(96), vb(80), ALU.mult))
                        vedge()
                    E("vector", lambda e, vb=vb: e.scalar_tensor_tensor(
                        vb(112), vb(32), -1.0, vb(96),
                        ALU.mult, ALU.mult))
                    chain_done = vmil()
                    W("scalar", sV, chain_done)
                    yield

                    t4m, gm = {}, {}
                    last_am = 0
                    for c in range(nchunks):
                        half = c % 2
                        if c >= 2:
                            W("scalar", sT, t4m[c - 2])
                        else:
                            NEED("scalar", "T4last")
                        E("scalar", lambda e, c=c, half=half: e.activation(
                            OLBD[:, half * 128:half * 128 + 128],
                            OB[:, c * 128:(c + 1) * 128], AF.Identity,
                            scale=VARB[:, 96 + c:97 + c],
                            bias=VARB[:, 112 + c:113 + c]).then_inc(sS.h, 1))
                        last_am = sS.inc()
                        PFX = PF[:, 0:128]
                        W("tensor", sS, last_am)
                        if c >= 1:
                            W("tensor", sV, gm[c - 1])
                        else:
                            NEED("tensor", "PF")
                        E("tensor", lambda e, half=half, P=PFX: e.transpose(
                            P, OLBD[:, half * 128:half * 128 + 128], IDB[:]))
                        t4m[c] = tmil()
                        W("vector", sT, t4m[c])
                        E("vector", lambda e, P=PFX: e.tensor_scalar(
                            OT[:, 128:256], P, LNCOL[:, 0:1], LNCOL[:, 1:2],
                            ALU.mult, ALU.add))
                        E("vector", lambda e, c=c: e.scalar_tensor_tensor(
                            OGS[:, c * CH:(c + 1) * CH],
                            GTH[:, c * CH:(c + 1) * CH], 1.0, OT[:, 128:256],
                            ALU.add, ALU.mult))
                        gm[c] = vmil()
                        if debug and c < 4:
                            E("vector", lambda e, c=c, half=half: e.tensor_copy(
                                DBGB[:, c * 272 + 128:c * 272 + 256],
                                OLBD[:, half * 128:half * 128 + 128]))
                            E("vector", lambda e, c=c: e.tensor_copy(
                                DBGB[:, c * 272 + 256:c * 272 + 257],
                                VARB[:, 96 + c:97 + c]))
                            E("vector", lambda e, c=c: e.tensor_copy(
                                DBGB[:, c * 272 + 257:c * 272 + 258],
                                VARB[:, 112 + c:113 + c]))
                        yield
                    MARK("PF", sV, gm[nchunks - 1])
                    MARK("GTHfree", sV, gm[nchunks - 1])
                    MARK("T4last", sT, t4m[nchunks - 1])

                    W("gpsimd", sV, gm[nchunks - 1])
                    E("gpsimd", lambda e, h=h: e.dma_start(
                        out=og_d[h * 128:(h + 1) * 128, :], in_=OGS[:])
                      .then_inc(sP.h, 16))
                    sP.inc(16)
                    MARK("spill", sP, sP.n)
                    NEED("vector", "spill")

                pending3 = gen_P3()

            rr(pending3)
            pending3 = None

            allspill = sP.n
            c_all_v, c_all_s, c_all_t, c_all_g = sV.n, sS.n, sT.n, sG.n
            for eng in ("tensor", "vector", "scalar", "gpsimd", "sync"):
                W(eng, sV, c_all_v)
                W(eng, sS, c_all_s)
                W(eng, sT, c_all_t)
                W(eng, sG, c_all_g)
                W(eng, sP, allspill)

            W("tensor", sWC, sWC.n)
            if not no_stage_d:
                W("sync", sT, sT.n)
                for ht in range(16):
                    E("sync", lambda e, ht=ht: e.dma_start(
                        out=OGT[:, ht * 2048:(ht + 1) * 2048],
                        in_=og_d[ht * 128:(ht + 1) * 128, :])
                      .then_inc(sOG.h, 16))
                    sOG.inc(16)
                og_ld = sOG.n
                W("tensor", sOG, og_ld)
            for mt in range(16 if not no_stage_d else 0):
                pi = mt % 2
                NEED("gpsimd", f"wos{pi}")
                sPar = sWa if pi == 0 else sWb
                E("gpsimd", lambda e, mt=mt, pi=pi, s=sPar: e.dma_start(
                    out=WOD[pi].rearrange("p (j c) -> p j c", j=16),
                    in_=wo_d[mt].rearrange("j p c -> p j c"))
                  .then_inc(s.h, 16))
                sPar.inc(16)
                W("tensor", sPar, sPar.n)
                for tq in range(4):
                    qi = (mt * 4 + tq) % 2
                    PSU = PA if qi == 0 else PB
                    NEED("tensor", "PA" if qi == 0 else "PB")
                    for ht in range(16):
                        E("tensor", lambda e, ht=ht, pi=pi, tq=tq, P=PSU:
                          e.matmul(
                            P[:], WOD[pi][:, ht * 128:(ht + 1) * 128],
                            OGT[:, ht * 2048 + tq * 512:
                                 ht * 2048 + (tq + 1) * 512],
                            start=(ht == 0), stop=(ht == 15)))
                    wom = tmil()
                    if tq == 3:
                        MARK(f"wos{pi}", sT, wom)
                    NEED("tensor", "PC", "PD")
                    E("tensor", lambda e, mt=mt, tq=tq: e.matmul(
                        PC[:, 0:256], EBS[:, mt * 128:(mt + 1) * 128],
                        MTS[:, tq * 512:tq * 512 + 256], start=True,
                        stop=True))
                    E("tensor", lambda e, mt=mt, tq=tq: e.matmul(
                        PD[:, 0:256], EBS[:, mt * 128:(mt + 1) * 128],
                        MTS[:, tq * 512 + 256:tq * 512 + 512], start=True,
                        stop=True))
                    ebm = tmil()
                    bi = (mt * 4 + tq) % 2
                    NEED("sync", f"xr{bi}")
                    E("sync", lambda e, mt=mt, tq=tq, bi=bi: e.dma_start(
                        out=XRT[bi][:],
                        in_=xt_d[mt * 128:(mt + 1) * 128,
                                 3 + tq * 512:3 + (tq + 1) * 512])
                      .then_inc(sXR.h, 16))
                    sXR.inc(16)
                    xr_ld = sXR.n
                    W("gpsimd", sV, sV.n)
                    E("gpsimd", lambda e, mt=mt, tq=tq, bi=bi: e.tensor_tensor(
                        FFN[bi][:], XBF[:, mt * TE + 3 + tq * 512:
                                        mt * TE + 3 + (tq + 1) * 512],
                        SWB[:, tq * 512:(tq + 1) * 512], ALU.mult)
                      .then_inc(sG.h, 1))
                    ffg = sG.inc()
                    W("vector", sG, ffg)
                    W("vector", sT, ebm)
                    E("vector", lambda e, bi=bi: e.tensor_tensor(
                        FFN[bi][:, 0:256], FFN[bi][:, 0:256], PC[:, 0:256],
                        ALU.add))
                    E("vector", lambda e, bi=bi: e.tensor_tensor(
                        FFN[bi][:, 256:512], FFN[bi][:, 256:512],
                        PD[:, 0:256], ALU.add))
                    ffv = vmil()
                    MARK("PC", sV, ffv)
                    MARK("PD", sV, ffv)
                    W("scalar", sV, ffv)
                    E("scalar", lambda e, bi=bi: e.activation(
                        THD[:], FFN[bi][:], AF.Tanh, scale=0.5)
                      .then_inc(sS.h, 1))
                    thf = sS.inc()
                    W("vector", sS, thf)
                    E("vector", lambda e: e.tensor_scalar(
                        THD[:], THD[:], 0.5, 0.5, ALU.mult, ALU.add))
                    E("vector", lambda e, bi=bi: e.tensor_tensor(
                        FFN[bi][:], FFN[bi][:], THD[:], ALU.mult))
                    W("vector", sT, wom)
                    W("vector", sXR, xr_ld)
                    W("vector", sO, sO.n)
                    E("vector", lambda e, bi=bi, P=PSU: e.tensor_tensor(
                        OUTS[bi][:], FFN[bi][:], P[:], ALU.mult))
                    E("vector", lambda e, bi=bi: e.tensor_tensor(
                        OUTS[bi][:], OUTS[bi][:], XRT[bi][:], ALU.add))
                    outv = vmil()
                    MARK("PA" if qi == 0 else "PB", sV, outv)
                    MARK(f"xr{bi}", sV, outv)
                    W("sync", sV, outv)
                    E("sync", lambda e, mt=mt, tq=tq, bi=bi: e.dma_start(
                        out=out_d[mt * 128:(mt + 1) * 128,
                                  tq * 512:(tq + 1) * 512],
                        in_=OUTS[bi][:]).then_inc(sO.h, 16))
                    sO.inc(16)
                    MARK(f"outdma{bi}", sO, sO.n)

            if debug:
                ddm = {"XBF": XBF, "MTS": MTS, "SWB": SWB, "ECOL": ECOL,
                       "LGAB": LGAB, "ST": ST, "MUR": MUR, "RSR": RSR,
                       "DBGB": DBGB, "KT": KT, "QT": QT, "VT": VT,
                       "GTH": GTH, "OGS": OGS, "UB": UB,
                       "WTS0": WTS[0], "U0S0": U0S[0], "PBT0": PBT[0],
                       "GKS0": GKS[0], "APK0": APK[0], "RS0": RS[0],
                       "KPRE": KPRE}
                for eng in ("sync",):
                    W(eng, sV, sV.n); W(eng, sS, sS.n)
                    W(eng, sT, sT.n); W(eng, sG, sG.n)
                for name, shape, dt in debug:
                    if name == "og":
                        E("sync", lambda e, d=dbg[name]: e.dma_start(
                            out=d, in_=og_d).then_inc(sO.h, 16))
                    else:
                        E("sync", lambda e, d=dbg[name], s=ddm[name]:
                          e.dma_start(out=d, in_=s[:]).then_inc(sO.h, 16))
                    sO.inc(16)
            W("sync", sO, sO.n)

            TENSORS.clear()
            TENSORS.update(dict(XBF=XBF, KT=KT, QT=QT, VT=VT, GTH=GTH,
                                OGS=OGS, MTS=MTS, SWB=SWB, LGAB=LGAB,
                                ECOL=ECOL, ST=ST, STB=STB, MUR=MUR, RSR=RSR,
                                KPRE=KPRE, QPRE=QPRE, VPRE=VPRE, UB=UB,
                                OT=OT, ET=ET, RECV=RECV, DIAG=DIAG,
                                PBT0=PBT[0], U0S0=U0S[0], WTS0=WTS[0],
                                GKS0=GKS[0], XS0=XS[0], XS1=XS[1],
                                RS0=RS[0], APK0=APK[0], EROWB=EROWB,
                                VARB=VARB, OLBD=OLBD, OB=OB, IDB=IDB,
                                CONVW=CONVW,
                                WSBq0=WSB["wq"][0], WSBk0=WSB["wk"][0],
                                WSBk1=WSB["wk"][1], WSBg1=WSB["wg"][1],
                                THB=THB, IDF=IDF))
            block = st.enter_context(nc.Block())

            @block.sync
            def _(eng):
                for f in em.q["sync"]:
                    f(eng)

            @block.tensor
            def _(eng):
                for f in em.q["tensor"]:
                    f(eng)

            @block.vector
            def _(eng):
                for f in em.q["vector"]:
                    f(eng)

            @block.scalar
            def _(eng):
                for f in em.q["scalar"]:
                    f(eng)

            @block.gpsimd
            def _(eng):
                for f in em.q["gpsimd"]:
                    f(eng)

    return nc


def kernel(**inputs):
    in_maps = _prep_inputs(inputs)
    nc = build_program()
    res = run_bass_kernel_spmd(nc, in_maps, core_ids=list(range(8)))
    out = np.zeros((4, 4096, 2048), np.float32)
    for b in range(4):
        for half in range(2):
            o = res.results[b * 2 + half]["outT"]
            out[b, half * TC:(half + 1) * TC, :] = o.T
    return out
